# version 04
# baseline (speedup 1.0000x reference)
import contextlib
import math
import numpy as np
import concourse.bass as bass
import concourse.mybir as mybir
from concourse.bass_utils import run_bass_kernel_spmd

F32 = mybir.dt.float32
BF16 = mybir.dt.bfloat16
I32 = mybir.dt.int32
AF = mybir.ActivationFunctionType
ALU = mybir.AluOpType
AX = mybir.AxisListType.X

NCORES = 8
B = 2
T = 2048
D = 1024
TOK = B * T
NT = TOK // 128

ENGS = ("sync", "scalar", "vector", "tensor", "gpsimd")
CMP_SEM = {"sync": "yc", "scalar": "sc", "vector": "vc", "tensor": "tn",
           "gpsimd": "gc"}
N_DMA_SEMS = 96


class Prog:
    def __init__(self, nc):
        self.nc = nc
        self.items = []
        self.cnt = {}
        self.last_wait = {}
        self.sems = {}
        self.prev_c = {}
        self.dma_keys = {}
        for sname in CMP_SEM.values():
            self.cnt[sname] = 0

    def dma_sem(self, key):
        if key not in self.dma_keys:
            i = len(self.dma_keys)
            assert i < N_DMA_SEMS, "out of dma sems"
            self.dma_keys[key] = f"dm{i}"
            self.cnt.setdefault(f"dm{i}", 0)
        return self.dma_keys[key]

    def e(self, eng, build, waits=(), sig=None, dkey=None):
        waits = list(waits)
        if sig == "c":
            waits.append(self.prev_c.get(eng))
        wl = []
        for tok in waits:
            if tok is None:
                continue
            sem, val, peng = tok
            key = (eng, sem)
            if self.last_wait.get(key, 0) >= val:
                continue
            self.last_wait[key] = val
            wl.append((sem, val))
        inc = None
        tok = None
        if sig == "dma":
            sem = self.dma_sem(dkey if dkey is not None else ("q", eng))
            self.cnt[sem] += 16
            inc = (sem, 16)
            tok = (sem, self.cnt[sem], eng)
        elif sig == "c":
            sem = CMP_SEM[eng]
            self.cnt[sem] += 1
            inc = (sem, 1)
            tok = (sem, self.cnt[sem], eng)
            self.prev_c[eng] = tok
        self.items.append((eng, wl, build, inc))
        return tok

    def barrier(self):
        for eng in ENGS:
            if eng == "sync":
                self.cnt["yc"] += 1
                self.items.append((
                    "sync", [],
                    lambda e, P=self: e.sem_inc(P.sems["yc"], 1), None))
            else:
                self.e(eng, lambda e: e.nop(), sig="c")
        toks = [(s, v, "#") for s, v in self.cnt.items()]
        for eng in ENGS:
            self.e(eng, None, waits=toks)

    def build_into(self, block):
        for eng in ENGS:
            mine = [(wl, b, inc) for (g, wl, b, inc) in self.items if g == eng]

            def emit(e, mine=mine):
                for wl, build, inc in mine:
                    for sem, val in wl:
                        e.wait_ge(self.sems[sem], val)
                    if build is None:
                        continue
                    ins = build(e)
                    if inc is not None and ins is not None:
                        ins.then_inc(self.sems[inc[0]], inc[1])
            getattr(block, eng)(emit)


class Arena:

    def __init__(self, nc, name, nbytes, dtype):
        self.dtype = dtype
        self.esz = 4 if dtype == F32 else 2
        self.n = nbytes // self.esz
        self.t = nc.alloc_sbuf_tensor(name, [128, self.n], dtype)
        self.off = 0
        self.marks = []

    def push(self):
        self.marks.append(self.off)

    def pop(self):
        self.off = self.marks.pop()

    def take(self, cols):
        assert self.off + cols <= self.n, \
            f"arena overflow: {self.off}+{cols} > {self.n}"
        ap = self.t[:, self.off:self.off + cols]
        self.off += cols
        return ap


def _dram(nc, name, shape, dtype=F32, kind="Internal"):
    return nc.dram_tensor(name, list(shape), dtype, kind=kind)


class Ctx:
    pass



def emit_consts(P, c):
    nc = P.nc
    c.identB = nc.alloc_sbuf_tensor("identB", [128, 128], BF16)
    c.identF = nc.alloc_sbuf_tensor("identF", [128, 128], F32)
    c.ones_col = nc.alloc_sbuf_tensor("ones_col", [128, 1], F32)
    c.ones_row = nc.alloc_sbuf_tensor("ones_row", [1, 128], F32)
    c.ones2k = nc.alloc_sbuf_tensor("ones2k", [128, 2048], F32)
    c.ltmask = nc.alloc_sbuf_tensor("ltmask", [128, 128], BF16)
    c.utmask = nc.alloc_sbuf_tensor("utmask", [128, 128], BF16)
    c.diagmask = nc.alloc_sbuf_tensor("diagmask", [128, 896], F32)

    P.e("gpsimd", lambda e: e.memset(c.ones_col[:], 1.0), sig="c")
    P.e("gpsimd", lambda e: e.memset(c.ones_row[:], 1.0), sig="c")
    P.e("gpsimd", lambda e: e.memset(c.ones2k[:], 1.0), sig="c")
    P.e("gpsimd", lambda e: e.iota(c.identF[:], pattern=[[-1, 128]], base=0,
                                   channel_multiplier=1,
                                   allow_small_or_imprecise_dtypes=True),
        sig="c")
    P.e("vector", lambda e: e.tensor_scalar(
        out=c.identB[:], in0=c.identF[:], scalar1=0.0, scalar2=None,
        op0=ALU.is_equal), waits=[P.prev_c["gpsimd"]], sig="c")
    tidF = P.e("vector", lambda e: e.tensor_copy(out=c.identF[:],
                                                 in_=c.identB[:]), sig="c")
    P.e("gpsimd", lambda e: e.iota(c.tmp2k[:, 0:128], pattern=[[-1, 128]],
                                   base=0, channel_multiplier=1,
                                   allow_small_or_imprecise_dtypes=True),
        waits=[tidF], sig="c")
    P.e("vector", lambda e: e.tensor_scalar(
        out=c.ltmask[:], in0=c.tmp2k[:, 0:128], scalar1=0.0, scalar2=None,
        op0=ALU.is_ge), waits=[P.prev_c["gpsimd"]], sig="c")
    P.e("vector", lambda e: e.tensor_scalar(
        out=c.utmask[:], in0=c.tmp2k[:, 0:128], scalar1=0.0, scalar2=None,
        op0=ALU.is_le), waits=[P.prev_c["gpsimd"]], sig="c")
    P.e("gpsimd", lambda e: e.iota(c.tmp2k[:, 0:896], pattern=[[-1, 896]],
                                   base=384, channel_multiplier=1,
                                   allow_small_or_imprecise_dtypes=True),
        waits=[P.prev_c["vector"]], sig="c")
    P.e("vector", lambda e: e.tensor_scalar(
        out=c.tmp2k[:, 0:896], in0=c.tmp2k[:, 0:896], scalar1=0.0,
        scalar2=None, op0=ALU.is_ge), waits=[P.prev_c["gpsimd"]], sig="c")
    c.tok_consts = P.e("vector", lambda e: e.tensor_scalar(
        out=c.diagmask[:], in0=c.tmp2k[:, 0:896], scalar1=1e30, scalar2=-1e30,
        op0=ALU.mult, op1=ALU.add), sig="c")


def constap(P, c, val):
    if val not in c.const_cache:
        idx = len(c.const_cache)
        ap = c.cpool[:, idx:idx + 1]
        tok = P.e("gpsimd", lambda e, ap=ap, val=val: e.memset(ap, float(val)),
                  sig="c")
        c.const_cache[val] = (ap, tok)
    return c.const_cache[val]


def ln_tile(P, c, xap, ytile, toks, slot=0, rms=False, eps=1e-5):
    st = c.ln_st[slot]
    sq = c.ln_sq[slot][:, :]
    war = c.ln_war.get(slot)
    tsq = P.e("scalar", lambda e: e.activation(out=sq, in_=xap, func=AF.Square),
              waits=list(toks) + ([war] if war else []), sig="c")
    t2 = P.e("vector", lambda e: e.reduce_sum(out=st[:, 1:2], in_=sq, axis=AX),
             waits=[tsq], sig="c")
    if rms:
        tvar = P.e("vector", lambda e: e.tensor_scalar(
            out=st[:, 2:3], in0=st[:, 1:2], scalar1=1.0 / D, scalar2=None,
            op0=ALU.mult), waits=[t2], sig="c")
    else:
        t1 = P.e("vector", lambda e: e.reduce_sum(out=st[:, 0:1], in_=xap,
                                                  axis=AX), waits=toks, sig="c")
        t3a = P.e("vector", lambda e: e.tensor_scalar(
            out=st[:, 3:4], in0=st[:, 0:1], scalar1=1.0 / D, scalar2=None,
            op0=ALU.mult), waits=[t1], sig="c")
        t3b = P.e("vector", lambda e: e.tensor_tensor(
            out=st[:, 4:5], in0=st[:, 3:4], in1=st[:, 3:4], op=ALU.mult),
            waits=[t3a], sig="c")
        t3c = P.e("vector", lambda e: e.tensor_scalar(
            out=st[:, 5:6], in0=st[:, 1:2], scalar1=1.0 / D, scalar2=None,
            op0=ALU.mult), waits=[t2], sig="c")
        tvar = P.e("vector", lambda e: e.tensor_tensor(
            out=st[:, 2:3], in0=st[:, 5:6], in1=st[:, 4:5], op=ALU.subtract),
            waits=[t3b, t3c], sig="c")
    eap, etok = constap(P, c, float(eps))
    t4 = P.e("scalar", lambda e: e.activation(out=st[:, 6:7], in_=st[:, 2:3],
                                              func=AF.Ln, bias=eap),
             waits=[tvar, etok], sig="c")
    t5 = P.e("scalar", lambda e: e.activation(out=st[:, 7:8], in_=st[:, 6:7],
                                              func=AF.Exp, scale=-0.5),
             waits=[t4], sig="c")
    if rms:
        tok = P.e("vector", lambda e: e.tensor_scalar(
            out=ytile, in0=xap, scalar1=st[:, 7:8], scalar2=None,
            op0=ALU.mult), waits=[t5] + list(toks), sig="c")
    else:
        tok = P.e("vector", lambda e: e.tensor_scalar(
            out=ytile, in0=xap, scalar1=st[:, 3:4], scalar2=st[:, 7:8],
            op0=ALU.subtract, op1=ALU.mult), waits=[t5] + list(toks), sig="c")
    c.ln_war[slot] = t2
    return tok


def fm_transpose(P, c, src_ap, dst_ap, waits, bank="B", scale_ap=None,
                 engcp="scalar", width=1024):
    off = 0 if bank == "A" else 1024
    nj = width // 128
    tt = None
    for j in range(nj):
        tt = P.e("tensor", lambda e, j=j: e.transpose(
            c.psumT[:, off + j * 128: off + (j + 1) * 128],
            src_ap[:, j * 128:(j + 1) * 128], c.identB[:]),
            waits=(list(waits) + [c.tok_consts, c.bank_tok.get(bank)])
            if j == 0 else [],
            sig="c" if j == nj - 1 else None)
    if scale_ap is not None:
        ev = P.e(engcp, lambda e: e.mul(out=dst_ap,
                                        in_=c.psumT[:, off:off + width],
                                        mul=scale_ap), waits=[tt], sig="c")
    else:
        ev = P.e(engcp, lambda e: e.copy(out=dst_ap,
                                         in_=c.psumT[:, off:off + width]),
                 waits=[tt], sig="c")
    c.bank_tok[bank] = ev
    return ev


def load_transposed_w(P, c, w_handle, dst, dkey, nat, n_otiles=8, n_d=8,
                      waits=(), w_off=0):
    No = n_otiles * 128
    Dw = n_d * 128
    tl = None
    for a in range(n_otiles):
        src = bass.AP(w_handle, w_off + a * 128 * Dw, [[Dw, 128], [1, Dw]])
        tl = P.e("gpsimd", lambda e, a=a, src=src: e.dma_start(
            out=nat[:, a * Dw:(a + 1) * Dw], in_=src),
            waits=list(waits), sig="dma", dkey=dkey)
    wtok = None
    last_tt = None
    for kd in range(n_d):
        for a0 in range(0, n_otiles, 8):
            na = min(8, n_otiles - a0)
            tt = None
            for ai in range(na):
                a = a0 + ai
                tt = P.e("tensor", lambda e, kd=kd, a=a, ai=ai: e.transpose(
                    c.psumT[:, ai * 128:(ai + 1) * 128],
                    nat[:, a * Dw + kd * 128: a * Dw + kd * 128 + 128],
                    c.identB[:]),
                    waits=[tl, c.tok_consts, c.bank_tok.get("A")]
                    if ai == 0 else [],
                    sig="c" if ai == na - 1 else None)
            ev = P.e("scalar", lambda e, kd=kd, a0=a0, na=na: e.copy(
                out=dst[:, kd * No + a0 * 128: kd * No + (a0 + na) * 128],
                in_=c.psumT[:, 0:na * 128]), waits=[tt], sig="c")
            c.bank_tok["A"] = ev
            wtok = ev
            last_tt = tt
    return wtok, last_tt



def emit_ternary_ffn(P, c, pre, x_src, x_dst):
    ins = c.ins
    A32, A16 = c.A32, c.A16
    A32.push()
    A16.push()
    bias_bc = A32.take(1024)
    small = A32.take(32)
    wT = A16.take(8192)
    A32.push()
    A16.push()
    wtile = A32.take(8192)
    wtq = A16.take(8192)

    aw = c.arena_war
    tW = None
    for a in range(8):
        wsrc = bass.AP(ins[pre + "weight"], a * 128 * 1024,
                       [[1024, 128], [1, 1024]])
        tW = P.e("gpsimd", lambda e, a=a, wsrc=wsrc: e.dma_start(
            out=wtile[:, a * 1024:(a + 1) * 1024], in_=wsrc),
            waits=[aw], sig="dma", dkey=("tw",))
    tb = P.e("sync", lambda e: e.dma_start(
        out=bias_bc, in_=bass.AP(ins[pre + "bias"], 0, [[0, 128], [1, 1024]])),
        waits=[aw], sig="dma", dkey=("tb",))
    ta = P.e("sync", lambda e: e.dma_start(
        out=small[:, 0:1], in_=bass.AP(ins[pre + "alpha"], 0,
                                       [[0, 128], [1, 1]])),
        waits=[aw], sig="dma", dkey=("ta",))
    ts = []
    for a in range(8):
        ts.append(P.e("vector", lambda e, a=a: e.reduce_sum(
            out=small[:, 2 + a:3 + a], in_=wtile[:, a * 1024:(a + 1) * 1024],
            axis=AX, apply_absolute_value=True), waits=[tW], sig="c"))
    t1 = P.e("vector", lambda e: e.reduce_sum(
        out=small[:, 10:11], in_=small[:, 2:10], axis=AX), waits=ts, sig="c")
    pm = c.psum[2]
    t2 = P.e("tensor", lambda e: e.matmul(pm[0:1, 0:1], c.ones_col[:],
                                          small[:, 10:11], start=True,
                                          stop=True),
             waits=[t1, c.tok_consts, c.bank_tok.get("p2")], sig="c")
    t3 = P.e("scalar", lambda e: e.activation(
        out=small[0:1, 11:12], in_=pm[0:1, 0:1], func=AF.Copy,
        scale=0.7 / (1024.0 * 1024.0)), waits=[t2], sig="c")
    t4 = P.e("tensor", lambda e: e.matmul(pm[0:128, 1:2], c.ones_row[:],
                                          small[0:1, 11:12], start=True,
                                          stop=True), waits=[t3], sig="c")
    t5 = P.e("vector", lambda e: e.tensor_copy(out=small[:, 20:21],
                                               in_=pm[0:128, 1:2]),
             waits=[t4], sig="c")
    c.bank_tok["p2"] = t5
    t6 = P.e("vector", lambda e: e.tensor_scalar(
        out=small[:, 21:22], in0=small[:, 20:21], scalar1=-1.0, scalar2=None,
        op0=ALU.mult), waits=[t5], sig="c")
    tq = []
    for a in range(8):
        sl = slice(a * 1024, (a + 1) * 1024)
        g1 = P.e("vector", lambda e, sl=sl: e.tensor_scalar(
            out=c.tmpA[:, 0:1024], in0=wtile[:, sl], scalar1=small[:, 20:21],
            scalar2=None, op0=ALU.is_gt), waits=[t5], sig="c")
        g2 = P.e("gpsimd", lambda e, sl=sl: e.tensor_scalar(
            out=c.tmpB[:, 0:1024], in0=wtile[:, sl], scalar1=small[:, 21:22],
            scalar2=None, op0=ALU.is_lt), waits=[t6] + tq[-1:], sig="c")
        tq.append(P.e("vector", lambda e, sl=sl: e.tensor_tensor(
            out=wtq[:, sl], in0=c.tmpA[:, 0:1024], in1=c.tmpB[:, 0:1024],
            op=ALU.subtract), waits=[g1, g2], sig="c"))
    wT_tok = None
    last_tt = None
    for kd in range(8):
        tt = None
        for a in range(8):
            tt = P.e("tensor", lambda e, kd=kd, a=a: e.transpose(
                c.psumT[:, a * 128:(a + 1) * 128],
                wtq[:, a * 1024 + kd * 128: a * 1024 + kd * 128 + 128],
                c.identB[:]),
                waits=[tq[a], c.tok_consts, c.bank_tok.get("A")]
                if a == 0 else [tq[a]],
                sig="c" if a == 7 else None)
        ev = P.e("scalar", lambda e, kd=kd: e.copy(
            out=wT[:, kd * 1024:(kd + 1) * 1024], in_=c.psumT[:, 0:1024]),
            waits=[tt], sig="c")
        c.bank_tok["A"] = ev
        wT_tok = ev
        last_tt = tt
    A32.pop()
    A16.pop()
    xin = A32.take(2048)
    yt = A32.take(2048)
    yn = A32.take(2048)
    xh = A16.take(2048)
    xhf = A16.take(2048)
    reuse_tok = [tq[-1], last_tt, wT_tok]

    prev_ev = {0: None, 1: None}
    prev_store = [None, None]
    prev_xuse = [None, None]
    for i in range(NT):
        sl = i % 2
        xi = xin[:, sl * 1024:(sl + 1) * 1024]
        xhs = xh[:, sl * 1024:(sl + 1) * 1024]
        xfs = xhf[:, sl * 1024:(sl + 1) * 1024]
        yts = yt[:, sl * 1024:(sl + 1) * 1024]
        yns = yn[:, sl * 1024:(sl + 1) * 1024]
        ti = P.e("sync", lambda e, i=i, xi=xi: e.dma_start(
            out=xi, in_=x_src.ap()[i * 128:(i + 1) * 128, :]),
            waits=[prev_xuse[sl]] + (reuse_tok if i < 2 else []),
            sig="dma", dkey=("xin", sl))
        tln = ln_tile(P, c, xi, xhs, [ti] + (reuse_tok if i < 2 else []),
                      slot=sl)
        evf = fm_transpose(P, c, xhs, xfs, [tln], bank="B")
        ytoks = []
        for oc in range(2):
            pmb = c.psum[oc]
            mm = None
            for kd in range(8):
                mm = P.e("tensor", lambda e, kd=kd, oc=oc, pmb=pmb, xfs=xfs:
                         e.matmul(pmb[:, 0:512],
                                  xfs[:, kd * 128:(kd + 1) * 128],
                                  wT[:, kd * 1024 + oc * 512:
                                     kd * 1024 + oc * 512 + 512],
                                  start=(kd == 0), stop=(kd == 7)),
                         waits=[evf, wT_tok, prev_ev[oc]],
                         sig="c" if kd == 7 else None)
            yk = P.e("vector", lambda e, oc=oc, pmb=pmb, yts=yts:
                     e.scalar_tensor_tensor(
                         out=yts[:, oc * 512:(oc + 1) * 512],
                         in0=pmb[:, 0:512], scalar=small[:, 0:1],
                         in1=bias_bc[:, oc * 512:(oc + 1) * 512],
                         op0=ALU.mult, op1=ALU.add),
                     waits=[mm, tb, ta] + (reuse_tok if i < 2 else []),
                     sig="c")
            prev_ev[oc] = yk
            ytoks.append(yk)
        tln2 = ln_tile(P, c, yts, yns, ytoks + [prev_store[sl]]
                       + (reuse_tok if i < 2 else []), slot=sl)
        tadd = P.e("vector", lambda e, yns=yns, xi=xi: e.tensor_tensor(
            out=yns, in0=yns, in1=xi, op=ALU.add), waits=[tln2], sig="c")
        prev_xuse[sl] = tadd
        st = P.e("sync", lambda e, i=i, yns=yns: e.dma_start(
            out=x_dst.ap()[i * 128:(i + 1) * 128, :], in_=yns),
            waits=[tadd], sig="dma", dkey=("xout", sl))
        prev_store[sl] = st
    A32.pop()
    A16.pop()
    P.barrier()
    c.arena_war = None



def emit_l3_attn(P, c, x_src, x_dst):
    ins = c.ins
    A32, A16 = c.A32, c.A16
    C7, tc7 = constap(P, c, 1e-7)
    C6, tc6 = constap(P, c, 1e-6)
    CN80, tn80 = constap(P, c, -80.0)
    C1, tc1 = constap(P, c, 1.0)
    aw = c.arena_war
    A32.push()
    A16.push()
    wrT = A16.take(8192)
    wiT = A16.take(8192)
    owT = A16.take(8192)
    xcb = A16.take(16384)
    cw = A32.take(32)
    vec = A32.take(64)
    norm_bc = c.tmpA[:, 0:1024]

    A16.push()
    nat = A16.take(8192)
    twr, r1 = load_transposed_w(P, c, ins["L3_attn_Wr_w"], wrT, ("l3w",),
                                nat, waits=[aw])
    twi, r2 = load_transposed_w(P, c, ins["L3_attn_Wi_w"], wiT, ("l3w",),
                                nat, waits=[r1])
    tow, r3 = load_transposed_w(P, c, ins["L3_attn_out_w"], owT, ("l3w",),
                                nat, waits=[r2])
    A16.pop()
    nat_free = r3

    tv = P.e("sync", lambda e: e.dma_start(
        out=cw, in_=bass.AP(ins["L3_attn_conv_weight"], 0,
                            [[4, 128], [512, 8], [1, 4]])),
        waits=[aw], sig="dma", dkey=("l3v",))
    tv = P.e("sync", lambda e: e.dma_start(
        out=vec[:, 0:8], in_=bass.AP(ins["L3_attn_conv_bias"], 0,
                                     [[1, 128], [128, 8], [1, 1]]),
        allow_slow_non_contiguous=True),
        waits=[aw], sig="dma", dkey=("l3v",))
    tv = P.e("sync", lambda e: e.dma_start(
        out=vec[:, 8:16], in_=bass.AP(ins["L3_attn_Wr_b"], 0,
                                      [[1, 128], [128, 8], [1, 1]]),
        allow_slow_non_contiguous=True),
        waits=[aw], sig="dma", dkey=("l3v",))
    tv = P.e("sync", lambda e: e.dma_start(
        out=vec[:, 16:24], in_=bass.AP(ins["L3_attn_Wi_b"], 0,
                                       [[1, 128], [128, 8], [1, 1]]),
        allow_slow_non_contiguous=True),
        waits=[aw], sig="dma", dkey=("l3v",))
    tv = P.e("sync", lambda e: e.dma_start(
        out=vec[:, 32:40], in_=bass.AP(ins["L3_attn_log_a"], 0,
                                       [[1, 128], [128, 8], [1, 1]]),
        allow_slow_non_contiguous=True),
        waits=[aw], sig="dma", dkey=("l3v",))
    tvec = tv
    tnw = P.e("sync", lambda e: e.dma_start(
        out=norm_bc, in_=bass.AP(ins["L3_attn_norm_w"], 0,
                                 [[0, 128], [1, 1024], [1, 1]])),
        waits=[aw], sig="dma", dkey=("l3nw",))
    t = P.e("vector", lambda e: e.tensor_scalar(
        out=vec[:, 40:48], in0=vec[:, 32:40], scalar1=-1.0, scalar2=None,
        op0=ALU.mult), waits=[tvec], sig="c")
    t = P.e("scalar", lambda e: e.activation(
        out=vec[:, 48:56], in_=vec[:, 40:48], func=AF.Abs), waits=[t], sig="c")
    t = P.e("scalar", lambda e: e.activation(
        out=vec[:, 48:56], in_=vec[:, 48:56], func=AF.Exp, scale=-1.0),
        waits=[t], sig="c")
    t = P.e("scalar", lambda e: e.activation(
        out=vec[:, 48:56], in_=vec[:, 48:56], func=AF.Ln, bias=C1),
        waits=[t, tc1], sig="c")
    t2 = P.e("vector", lambda e: e.tensor_scalar(
        out=vec[:, 56:64], in0=vec[:, 40:48], scalar1=0.0, scalar2=None,
        op0=ALU.max), waits=[t], sig="c")
    t = P.e("vector", lambda e: e.tensor_tensor(
        out=vec[:, 24:32], in0=vec[:, 48:56], in1=vec[:, 56:64], op=ALU.add),
        waits=[t2, t], sig="c")
    ts8 = P.e("vector", lambda e: e.tensor_scalar(
        out=vec[:, 24:32], in0=vec[:, 24:32], scalar1=-8.0, scalar2=None,
        op0=ALU.mult), waits=[t], sig="c")

    prev_xuse = [None, None]
    prev_store = [None, None]
    xcb_war = None
    last_out_mm = None
    hfm_free = nat_free
    for b in range(B):
        A16.push()
        xf = A16.take(16384)
        xhb = A16.take(2048)
        A32.push()
        xin = A32.take(2048)
        ev_done = []
        for it in range(16):
            i = b * 16 + it
            sl = it % 2
            xi = xin[:, sl * 1024:(sl + 1) * 1024]
            xbs = xhb[:, sl * 1024:(sl + 1) * 1024]
            ti = P.e("sync", lambda e, i=i, xi=xi: e.dma_start(
                out=xi, in_=x_src.ap()[i * 128:(i + 1) * 128, :]),
                waits=[prev_xuse[sl]] + ([hfm_free] if it < 2 else []),
                sig="dma", dkey=("xin", sl))
            tln = ln_tile(P, c, xi, xbs,
                          [ti] + ([hfm_free, xcb_war] if it < 2 else []),
                          slot=sl)
            prev_xuse[sl] = tln
            for half in range(2):
                poff = 1024 if half == 0 else 0
                bkey = "B" if half == 0 else "A"
                tt = None
                for j in range(4):
                    jj = half * 4 + j
                    tt = P.e("tensor", lambda e, jj=jj, j=j, poff=poff,
                             xbs=xbs: e.transpose(
                                 c.psumT[:, poff + j * 128:
                                         poff + (j + 1) * 128],
                                 xbs[:, jj * 128:(jj + 1) * 128],
                                 c.identB[:]),
                             waits=([tln, c.tok_consts,
                                     c.bank_tok.get(bkey)]
                                    + ([hfm_free] if it < 2 else []))
                             if j == 0 else [],
                             sig="c" if j == 3 else None)
                dsth = bass.AP(xf.tensor, xf.offset + (half * 4) * 2048
                               + it * 128,
                               [[A16.n, 128], [2048, 4], [1, 128]])
                ev = P.e("scalar", lambda e, poff=poff, dsth=dsth: e.copy(
                    out=dsth, in_=c.psumT[:, poff:poff + 512]),
                    waits=[tt], sig="c")
                c.bank_tok[bkey] = ev
                ev_done.append(ev)
        t_cast = None
        t_xf_free = None
        for dt in range(8):
            xfd = xf[:, dt * 2048:(dt + 1) * 2048]
            acc = c.tmp2k[:, :]
            t = P.e("vector", lambda e, dt=dt: e.tensor_scalar(
                out=acc, in0=c.ones2k[:, 0:2048], scalar1=vec[:, dt:dt + 1],
                scalar2=None, op0=ALU.mult),
                waits=[tvec, c.tok_consts, t_cast]
                + (ev_done + [xcb_war] if dt == 0 else []), sig="c")
            for k in range(4):
                sft = 3 - k
                t = P.e("vector", lambda e, dt=dt, k=k, sft=sft, xfd=xfd,
                        acc=acc: e.scalar_tensor_tensor(
                            out=acc[:, sft:2048], in0=xfd[:, 0:2048 - sft],
                            scalar=cw[:, dt * 4 + k:dt * 4 + k + 1],
                            in1=acc[:, sft:2048], op0=ALU.mult, op1=ALU.add),
                        waits=[t], sig="c")
            t_xf_free = t
            t_cast = P.e("scalar", lambda e, dt=dt, acc=acc: e.copy(
                out=xcb[:, dt * 2048:(dt + 1) * 2048], in_=acc),
                waits=[t], sig="c")
        txcb = t_cast
        A32.pop()
        A16.pop()

        A16.push()
        hfm = A16.take(16384)
        A32.push()
        w1 = A32.take(2048)
        w2 = A32.take(2048)
        w3 = A32.take(2048)
        w4 = A32.take(2048)
        w5 = A32.take(2048)
        hdone = []
        for dt in range(8):
            ev_r = None
            ev_i = None
            for which in ("r", "i"):
                wTv = wrT if which == "r" else wiT
                wtok = twr if which == "r" else twi
                bsl = vec[:, 8 + dt:9 + dt] if which == "r" \
                    else vec[:, 16 + dt:17 + dt]
                dst = w1 if which == "r" else w2
                pb = 0 if which == "r" else 1
                for tc in range(4):
                    mm = None
                    for kd in range(8):
                        mm = P.e("tensor", lambda e, kd=kd, tc=tc, dt=dt,
                                 wTv=wTv, pb=pb: e.matmul(
                                     c.psum[pb][:, 0:512],
                                     wTv[:, kd * 1024 + dt * 128:
                                         kd * 1024 + dt * 128 + 128],
                                     xcb[:, kd * 2048 + tc * 512:
                                         kd * 2048 + tc * 512 + 512],
                                     start=(kd == 0), stop=(kd == 7)),
                                 waits=([txcb, wtok,
                                         c.bank_tok.get("pb%d" % pb)]
                                        + ([t_xf_free, hfm_free]
                                           if dt == 0 and tc == 0 else []))
                                 if kd == 0 else [],
                                 sig="c" if kd == 7 else None)
                    ev = P.e("scalar", lambda e, tc=tc, dst=dst, pb=pb,
                             bsl=bsl: e.activation(
                                 out=dst[:, tc * 512:(tc + 1) * 512],
                                 in_=c.psum[pb][:, 0:512], func=AF.Sigmoid,
                                 bias=bsl),
                             waits=[mm, tvec]
                             + ([prev_store[0], prev_store[1]]
                                if dt == 0 and tc == 0 else []), sig="c")
                    c.bank_tok["pb%d" % pb] = ev
                    if which == "r":
                        ev_r = ev
                    else:
                        ev_i = ev
            xcd = xcb[:, dt * 2048:(dt + 1) * 2048]
            t_a = P.e("scalar", lambda e, dt=dt: e.activation(
                out=w3, in_=w1, func=AF.Exp, scale=vec[:, 24 + dt:25 + dt]),
                waits=[ev_r, ts8] + (hdone[-1:] if hdone else []), sig="c")
            t_ix = P.e("vector", lambda e, xcd=xcd: e.tensor_tensor(
                out=w4, in0=w2, in1=xcd, op=ALU.mult),
                waits=[ev_i, txcb] + ([last_out_mm] if dt == 0 else []),
                sig="c")
            t_a2 = P.e("vector", lambda e: e.tensor_tensor(
                out=w2, in0=w3, in1=w3, op=ALU.mult),
                waits=[t_a, t_ix], sig="c")
            t_t2 = P.e("vector", lambda e: e.tensor_scalar(
                out=w5, in0=w2, scalar1=-1.0, scalar2=1.0,
                op0=ALU.mult, op1=ALU.add), waits=[t_a2], sig="c")
            t_t2 = P.e("vector", lambda e: e.tensor_scalar(
                out=w5, in0=w5, scalar1=C6, scalar2=None, op0=ALU.max),
                waits=[t_t2, tc6], sig="c")
            t_l2 = P.e("scalar", lambda e: e.activation(
                out=w2, in_=w5, func=AF.Ln), waits=[t_t2], sig="c")
            t_sq = P.e("scalar", lambda e: e.activation(
                out=w5, in_=w2, func=AF.Exp, scale=0.5), waits=[t_l2], sig="c")
            t_bt = P.e("vector", lambda e: e.tensor_tensor(
                out=w4, in0=w5, in1=w4, op=ALU.mult), waits=[t_sq], sig="c")
            t_m = P.e("vector", lambda e: e.tensor_scalar(
                out=w5, in0=w3, scalar1=C7, scalar2=None, op0=ALU.max),
                waits=[t_bt, tc7], sig="c")
            t_la = P.e("scalar", lambda e: e.activation(
                out=w2, in_=w5, func=AF.Ln), waits=[t_m], sig="c")
            t_cs = P.e("vector", lambda e: e.tensor_tensor_scan(
                out=w3, data0=c.ones2k[:, :], data1=w2, initial=0.0,
                op0=ALU.mult, op1=ALU.add),
                waits=[t_la, c.tok_consts], sig="c")
            t_lA = P.e("vector", lambda e: e.tensor_scalar(
                out=w2, in0=w3, scalar1=CN80, scalar2=None, op0=ALU.max),
                waits=[t_cs, tn80], sig="c")
            t_A = P.e("scalar", lambda e: e.activation(
                out=w3, in_=w2, func=AF.Exp), waits=[t_lA], sig="c")
            t_en = P.e("scalar", lambda e: e.activation(
                out=w5, in_=w2, func=AF.Exp, scale=-1.0), waits=[t_A], sig="c")
            t_u = P.e("vector", lambda e: e.tensor_tensor(
                out=w4, in0=w4, in1=w5, op=ALU.mult), waits=[t_en], sig="c")
            t_cu = P.e("vector", lambda e: e.tensor_tensor_scan(
                out=w5, data0=c.ones2k[:, :], data1=w4, initial=0.0,
                op0=ALU.mult, op1=ALU.add), waits=[t_u], sig="c")
            t_h = P.e("vector", lambda e, dt=dt: e.tensor_tensor(
                out=hfm[:, dt * 2048:(dt + 1) * 2048], in0=w3, in1=w5,
                op=ALU.mult), waits=[t_cu, t_A], sig="c")
            hdone.append(t_h)
        xcb_war = hdone[-1]
        A32.pop()

        A32.push()
        yt = A32.take(2048)
        yn = A32.take(2048)
        xe = A32.take(2048)
        scan_free = hdone[-1]
        for it in range(16):
            i = b * 16 + it
            sl = it % 2
            yts = yt[:, sl * 1024:(sl + 1) * 1024]
            yns = yn[:, sl * 1024:(sl + 1) * 1024]
            xes = xe[:, sl * 1024:(sl + 1) * 1024]
            txe = P.e("sync", lambda e, i=i, xes=xes: e.dma_start(
                out=xes, in_=x_src.ap()[i * 128:(i + 1) * 128, :]),
                waits=[prev_store[sl]] + ([scan_free] if it < 2 else []),
                sig="dma", dkey=("xep", sl))
            ytoks = []
            for oc in range(2):
                pb = 2 + oc
                mm = None
                for kd in range(8):
                    mm = P.e("tensor", lambda e, kd=kd, oc=oc, it=it, pb=pb:
                             e.matmul(
                                 c.psum[pb][:, 0:512],
                                 hfm[:, kd * 2048 + it * 128:
                                     kd * 2048 + it * 128 + 128],
                                 owT[:, kd * 1024 + oc * 512:
                                     kd * 1024 + oc * 512 + 512],
                                 start=(kd == 0), stop=(kd == 7)),
                             waits=(hdone + [tow,
                                             c.bank_tok.get("pb%d" % pb)])
                             if kd == 0 else [],
                             sig="c" if kd == 7 else None)
                last_out_mm = mm
                ev = P.e("vector", lambda e, oc=oc, pb=pb, yts=yts:
                         e.tensor_copy(out=yts[:, oc * 512:(oc + 1) * 512],
                                       in_=c.psum[pb][:, 0:512]),
                         waits=[mm] + ([scan_free] if it < 2 else []),
                         sig="c")
                c.bank_tok["pb%d" % pb] = ev
                ytoks.append(ev)
            t1 = ln_tile(P, c, yts, yns,
                         ytoks + [prev_store[sl]]
                         + ([scan_free] if it < 2 else []),
                         slot=sl, rms=True, eps=1e-6)
            t2 = P.e("vector", lambda e, yns=yns: e.tensor_tensor(
                out=yns, in0=yns, in1=norm_bc, op=ALU.mult),
                waits=[t1, tnw], sig="c")
            t3 = ln_tile(P, c, yns, yns, [t2], slot=sl)
            t4 = P.e("vector", lambda e, yns=yns, xes=xes: e.tensor_tensor(
                out=yns, in0=yns, in1=xes, op=ALU.add),
                waits=[t3, txe], sig="c")
            st = P.e("sync", lambda e, i=i, yns=yns: e.dma_start(
                out=x_dst.ap()[i * 128:(i + 1) * 128, :], in_=yns),
                waits=[t4], sig="dma", dkey=("xout", sl))
            prev_store[sl] = st
        A32.pop()
        A16.pop()
        hfm_free = last_out_mm
    A32.pop()
    A16.pop()
    P.barrier()
    c.arena_war = None





def emit_l1_attn(P, c, x_src, x_dst):
    ins = c.ins
    A32, A16 = c.A32, c.A16
    C6, tc6 = constap(P, c, 1e-6)
    C1, tc1 = constap(P, c, 1.0)
    aw = c.arena_war
    scr = _dram(P.nc, "l1scr", (1, 16), F32)
    A32.push()
    A16.push()
    qkvT = A16.take(24 * 1024)
    projT = A16.take(8 * 1024)
    sc_bc = A32.take(16)
    small = A32.take(128)
    rd = A32.take(16)

    A16.push()
    nat = A16.take(24 * 1024)
    tqw, r1 = load_transposed_w(P, c, ins["L1_attn_qkv_w"], qkvT, ("l1w",),
                                nat, n_otiles=24, waits=[aw])
    tpw, r2 = load_transposed_w(P, c, ins["L1_attn_proj_w"], projT, ("l1w",),
                                nat, n_otiles=8, waits=[r1])
    A16.pop()
    nat_free = r2

    tal = P.e("sync", lambda e: e.dma_start(
        out=small[0:1, 0:16], in_=ins["L1_attn_alpha_log"].ap()[None, :]),
        waits=[aw], sig="dma", dkey=("l1v",))
    tv = P.e("sync", lambda e: e.dma_start(
        out=small[0:1, 16:32], in_=ins["L1_attn_beta_log"].ap()[None, :]),
        waits=[aw], sig="dma", dkey=("l1v",))
    t = P.e("scalar", lambda e: e.activation(
        out=small[0:1, 32:64], in_=small[0:1, 0:32], func=AF.Abs),
        waits=[tv], sig="c")
    t = P.e("scalar", lambda e: e.activation(
        out=small[0:1, 32:64], in_=small[0:1, 32:64], func=AF.Exp, scale=-1.0),
        waits=[t], sig="c")
    t = P.e("scalar", lambda e: e.activation(
        out=small[0:1, 32:64], in_=small[0:1, 32:64], func=AF.Ln,
        bias=C1[0:1, :]), waits=[t, tc1], sig="c")
    tm = P.e("vector", lambda e: e.tensor_scalar(
        out=small[0:1, 64:96], in0=small[0:1, 0:32], scalar1=0.0,
        scalar2=None, op0=ALU.max), waits=[tv, t], sig="c")
    t = P.e("vector", lambda e: e.tensor_tensor(
        out=small[0:1, 64:96], in0=small[0:1, 64:96], in1=small[0:1, 32:64],
        op=ALU.add), waits=[t, tm], sig="c")
    t = P.e("vector", lambda e: e.tensor_scalar(
        out=small[0:1, 64:96], in0=small[0:1, 64:96], scalar1=1e-6,
        scalar2=None, op0=ALU.add), waits=[t], sig="c")
    t = P.e("vector", lambda e: e.tensor_tensor(
        out=small[0:1, 64:80], in0=small[0:1, 64:80], in1=small[0:1, 80:96],
        op=ALU.mult), waits=[t], sig="c")
    t = P.e("vector", lambda e: e.tensor_scalar(
        out=small[0:1, 64:80], in0=small[0:1, 64:80], scalar1=8.0,
        scalar2=None, op0=ALU.mult), waits=[t], sig="c")
    tsc = P.e("sync", lambda e: e.dma_start(out=scr.ap()[0:1, :],
                                            in_=small[0:1, 64:80]),
              waits=[t], sig="dma", dkey=("l1sc",))
    tscb = P.e("sync", lambda e: e.dma_start(
        out=sc_bc, in_=bass.AP(scr, 0, [[0, 128], [1, 16]])),
        waits=[tsc], sig="dma", dkey=("l1sc",))

    xin = A32.take(2048)
    yts = A32.take(1024)
    yns = A32.take(1024)
    xep = A32.take(2048)
    poscol = A32.take(4)
    S32 = A32.take(512)
    xhat = A16.take(4096)
    xhb = A16.take(2048)
    phiq = A16.take(4096)
    phik = A16.take(4096)
    ktm = A16.take(4096)
    vtm = A16.take(4096)
    sprev = A16.take(512)
    Asb = A16.take(128)
    yfm = A16.take(1024)
    yck = A16.take(1024)

    qk_banks = ["p2", "p5"]
    qk_psum = [c.psum[2], c.psum[5]]
    prev_xuse = [None, None]
    prev_store = [None, None]
    prev_phi_use = [nat_free]
    prev_epi = None
    for b in range(B):
        for sc_i in range(4):
            ev_done = []
            for it4 in range(4):
                it = sc_i * 4 + it4
                i = b * 16 + it
                sl = it % 2
                xi = xin[:, sl * 1024:(sl + 1) * 1024]
                xbs = xhb[:, sl * 1024:(sl + 1) * 1024]
                ti = P.e("sync", lambda e, i=i, xi=xi: e.dma_start(
                    out=xi, in_=x_src.ap()[i * 128:(i + 1) * 128, :]),
                    waits=[prev_xuse[sl], nat_free], sig="dma",
                    dkey=("xin", sl))
                tln = ln_tile(P, c, xi, xbs, [ti], slot=sl)
                prev_xuse[sl] = tln
                for half in range(2):
                    poff = 1024 if half == 0 else 0
                    bkey = "B" if half == 0 else "A"
                    tt = None
                    for j in range(4):
                        jj = half * 4 + j
                        tt = P.e("tensor", lambda e, jj=jj, j=j, poff=poff,
                                 xbs=xbs: e.transpose(
                                     c.psumT[:, poff + j * 128:
                                             poff + (j + 1) * 128],
                                     xbs[:, jj * 128:(jj + 1) * 128],
                                     c.identB[:]),
                                 waits=([tln, c.tok_consts,
                                         c.bank_tok.get(bkey)]
                                        + prev_phi_use) if j == 0 else [],
                                 sig="c" if j == 3 else None)
                    dsth = bass.AP(xhat.tensor,
                                   xhat.offset + (half * 4) * 512 + it4 * 128,
                                   [[A16.n, 128], [512, 4], [1, 128]])
                    ev = P.e("scalar", lambda e, poff=poff, dsth=dsth: e.copy(
                        out=dsth, in_=c.psumT[:, poff:poff + 512]),
                        waits=[tt], sig="c")
                    c.bank_tok[bkey] = ev
                    ev_done.append(ev)
            gi = 0
            evs_qkv = []
            for which in ("q", "k"):
                obase = 0 if which == "q" else 8
                dstb = phiq if which == "q" else phik
                for ot in range(8):
                    pb = gi % 2
                    mm = None
                    for kd in range(8):
                        mm = P.e("tensor", lambda e, kd=kd, ot=ot, pb=pb,
                                 obase=obase: e.matmul(
                                     qk_psum[pb][:, 0:512],
                                     qkvT[:, kd * 3072 + (obase + ot) * 128:
                                          kd * 3072 + (obase + ot) * 128
                                          + 128],
                                     xhat[:, kd * 512:(kd + 1) * 512],
                                     start=(kd == 0), stop=(kd == 7)),
                                 waits=([tqw,
                                         c.bank_tok.get(qk_banks[pb])]
                                        + ev_done) if kd == 0 else [],
                                 sig="c" if kd == 7 else None)
                    ev = P.e("vector", lambda e, ot=ot, pb=pb, dstb=dstb:
                             e.tensor_scalar(
                                 out=dstb[:, ot * 512:(ot + 1) * 512],
                                 in0=qk_psum[pb][:, 0:512], scalar1=0.0,
                                 scalar2=C6, op0=ALU.max, op1=ALU.add),
                             waits=[mm, tc6] + prev_phi_use, sig="c")
                    c.bank_tok[qk_banks[pb]] = ev
                    evs_qkv.append(ev)
                    gi += 1
            for which in ("kt", "v"):
                obase = 1024 if which == "kt" else 2048
                dstb = ktm if which == "kt" else vtm
                for tt4 in range(4):
                    for ocx in range(2):
                        pb = gi % 2
                        mm = None
                        for kd in range(8):
                            mm = P.e("tensor", lambda e, kd=kd, tt4=tt4,
                                     ocx=ocx, pb=pb, obase=obase: e.matmul(
                                         qk_psum[pb][:, 0:512],
                                         xhat[:, kd * 512 + tt4 * 128:
                                              kd * 512 + tt4 * 128 + 128],
                                         qkvT[:, kd * 3072 + obase
                                              + ocx * 512:
                                              kd * 3072 + obase + ocx * 512
                                              + 512],
                                         start=(kd == 0), stop=(kd == 7)),
                                     waits=[tqw,
                                            c.bank_tok.get(qk_banks[pb])]
                                     if kd == 0 else [],
                                     sig="c" if kd == 7 else None)
                        if which == "kt":
                            ev = P.e("vector", lambda e, tt4=tt4, ocx=ocx,
                                     pb=pb, dstb=dstb: e.tensor_scalar(
                                         out=dstb[:, tt4 * 1024 + ocx * 512:
                                                  tt4 * 1024 + ocx * 512
                                                  + 512],
                                         in0=qk_psum[pb][:, 0:512],
                                         scalar1=0.0, scalar2=C6,
                                         op0=ALU.max, op1=ALU.add),
                                     waits=[mm, tc6] + prev_phi_use,
                                     sig="c")
                        else:
                            ev = P.e("vector", lambda e, tt4=tt4, ocx=ocx,
                                     pb=pb, dstb=dstb: e.tensor_copy(
                                         out=dstb[:, tt4 * 1024 + ocx * 512:
                                                  tt4 * 1024 + ocx * 512
                                                  + 512],
                                         in_=qk_psum[pb][:, 0:512]),
                                     waits=[mm] + prev_phi_use, sig="c")
                        c.bank_tok[qk_banks[pb]] = ev
                        evs_qkv.append(ev)
                        gi += 1
            qkv_ready = evs_qkv[-1]

            for cc in range(4):
                g = sc_i * 4 + cc
                it = g
                i = b * 16 + it
                sl = it % 2
                tpos = P.e("gpsimd", lambda e, g=g: e.iota(
                    poscol[:, 0:1], pattern=[[0, 1]], base=g * 128 + 1,
                    channel_multiplier=1,
                    allow_small_or_imprecise_dtypes=True),
                    waits=[prev_epi], sig="c")
                trd = P.e("vector", lambda e: e.tensor_scalar(
                    out=rd, in0=sc_bc, scalar1=poscol[:, 0:1], scalar2=None,
                    op0=ALU.mult), waits=[tpos, tscb, prev_epi], sig="c")
                trd = P.e("vector", lambda e: e.tensor_scalar(
                    out=rd, in0=rd, scalar1=C6, scalar2=None, op0=ALU.max),
                    waits=[trd, tc6], sig="c")
                trd = P.e("vector", lambda e: e.reciprocal(out=rd, in_=rd),
                          waits=[trd], sig="c")
                out_mm_tok = [None, None]
                smm_tok = None
                sprev_tok = c.bank_tok.get("se")
                for h in range(16):
                    rs = (h % 2) * 64
                    csl = slice((h // 2) * 512 + cc * 128,
                                (h // 2) * 512 + cc * 128 + 128)
                    amm = P.e("tensor", lambda e, h=h, rs=rs, csl=csl:
                              e.matmul(c.psum[5][:, 0:128],
                                       phik[rs:rs + 64, csl],
                                       phiq[rs:rs + 64, csl],
                                       start=True, stop=True),
                              waits=[qkv_ready, c.bank_tok.get("p5"),
                                     c.tok_consts], sig="c")
                    aev = P.e("vector", lambda e: e.tensor_tensor(
                        out=Asb, in0=c.psum[5][:, 0:128], in1=c.utmask[:],
                        op=ALU.mult), waits=[amm], sig="c")
                    c.bank_tok["p5"] = aev
                    opb = 3 + h // 8
                    ocol = (h % 8) * 64
                    if g > 0:
                        P.e("tensor", lambda e, h=h, rs=rs, csl=csl, opb=opb,
                            ocol=ocol: e.matmul(
                                c.psum[opb][:, ocol:ocol + 64],
                                phiq[rs:rs + 64, csl],
                                sprev[rs:rs + 64,
                                      (h // 2) * 64:(h // 2) * 64 + 64],
                                start=True, stop=False),
                            waits=[qkv_ready, sprev_tok,
                                   c.bank_tok.get("pb%d" % opb)], sig=None)
                        st1 = False
                    else:
                        st1 = True
                    mm2 = P.e("tensor", lambda e, h=h, cc=cc, opb=opb,
                              ocol=ocol, st1=st1: e.matmul(
                                  c.psum[opb][:, ocol:ocol + 64], Asb,
                                  vtm[:, cc * 1024 + h * 64:
                                      cc * 1024 + h * 64 + 64],
                                  start=st1, stop=True),
                              waits=[aev, qkv_ready,
                                     c.bank_tok.get("pb%d" % opb)], sig="c")
                    out_mm_tok[h // 8] = mm2
                    spb = h // 8
                    srow = (h % 2) * 64
                    scol = ((h % 8) // 2) * 64
                    smm = P.e("tensor", lambda e, h=h, cc=cc, spb=spb,
                              srow=srow, scol=scol: e.matmul(
                                  c.psum[spb][srow:srow + 64,
                                              scol:scol + 64],
                                  ktm[:, cc * 1024 + h * 64:
                                      cc * 1024 + h * 64 + 64],
                                  vtm[:, cc * 1024 + h * 64:
                                      cc * 1024 + h * 64 + 64],
                                  start=True, stop=True),
                              waits=[qkv_ready,
                                     c.bank_tok.get("ps%d" % spb)]
                              if h % 8 == 0 else [],
                              sig="c" if h % 8 == 7 else None)
                    if h % 8 == 7:
                        smm_tok = smm
                if g == 0:
                    sa0 = P.e("vector", lambda e: e.tensor_copy(
                        out=S32[:, 0:256], in_=c.psum[0][:, 0:256]),
                        waits=[smm_tok], sig="c")
                    sa1 = P.e("vector", lambda e: e.tensor_copy(
                        out=S32[:, 256:512], in_=c.psum[1][:, 0:256]),
                        waits=[smm_tok], sig="c")
                else:
                    sa0 = P.e("vector", lambda e: e.tensor_tensor(
                        out=S32[:, 0:256], in0=S32[:, 0:256],
                        in1=c.psum[0][:, 0:256], op=ALU.add),
                        waits=[smm_tok], sig="c")
                    sa1 = P.e("vector", lambda e: e.tensor_tensor(
                        out=S32[:, 256:512], in0=S32[:, 256:512],
                        in1=c.psum[1][:, 0:256], op=ALU.add),
                        waits=[smm_tok], sig="c")
                se1 = P.e("vector", lambda e: e.tensor_copy(
                    out=sprev, in_=S32), waits=[sa0, sa1], sig="c")
                c.bank_tok["ps0"] = sa0
                c.bank_tok["ps1"] = sa1
                c.bank_tok["se"] = se1
                yev = None
                for h in range(16):
                    opb = 3 + h // 8
                    ocol = (h % 8) * 64
                    yev = P.e("vector", lambda e, h=h, opb=opb, ocol=ocol:
                              e.tensor_scalar(
                                  out=yck[:, h * 64:(h + 1) * 64],
                                  in0=c.psum[opb][:, ocol:ocol + 64],
                                  scalar1=rd[:, h:h + 1], scalar2=None,
                                  op0=ALU.mult),
                              waits=[out_mm_tok[h // 8], trd, prev_epi],
                              sig="c")
                c.bank_tok["pb3"] = yev
                c.bank_tok["pb4"] = yev
                tyf = fm_transpose(P, c, yck, yfm, [yev], bank="B")
                txe = P.e("sync", lambda e, i=i, sl=sl: e.dma_start(
                    out=xep[:, sl * 1024:(sl + 1) * 1024],
                    in_=x_src.ap()[i * 128:(i + 1) * 128, :]),
                    waits=[prev_store[sl]], sig="dma", dkey=("xep", sl))
                ptoks = []
                for ocx in range(2):
                    pbank = qk_psum[ocx]
                    bk = qk_banks[ocx]
                    mm = None
                    for kd in range(8):
                        mm = P.e("tensor", lambda e, kd=kd, ocx=ocx,
                                 pbank=pbank: e.matmul(
                                     pbank[:, 0:512],
                                     yfm[:, kd * 128:(kd + 1) * 128],
                                     projT[:, kd * 1024 + ocx * 512:
                                           kd * 1024 + ocx * 512 + 512],
                                     start=(kd == 0), stop=(kd == 7)),
                                 waits=[tyf, tpw, c.bank_tok.get(bk)]
                                 if kd == 0 else [],
                                 sig="c" if kd == 7 else None)
                    ev = P.e("vector", lambda e, ocx=ocx, pbank=pbank:
                             e.tensor_copy(
                                 out=yts[:, ocx * 512:(ocx + 1) * 512],
                                 in_=pbank[:, 0:512]),
                             waits=[mm, prev_epi], sig="c")
                    c.bank_tok[bk] = ev
                    ptoks.append(ev)
                t3 = ln_tile(P, c, yts, yns,
                             ptoks + [prev_store[0], prev_store[1]],
                             slot=sl)
                t4 = P.e("vector", lambda e, sl=sl: e.tensor_tensor(
                    out=yns, in0=yns,
                    in1=xep[:, sl * 1024:(sl + 1) * 1024], op=ALU.add),
                    waits=[t3, txe], sig="c")
                stt = P.e("sync", lambda e, i=i, sl=sl: e.dma_start(
                    out=x_dst.ap()[i * 128:(i + 1) * 128, :], in_=yns),
                    waits=[t4], sig="dma", dkey=("xout", sl))
                prev_store[sl] = stt
                prev_epi = t4
            prev_phi_use = [smm_tok]
    A32.pop()
    A16.pop()
    P.barrier()
    c.arena_war = None



def transpose_w_to_dram(P, c, w_handle, dst_dram, e_idx, n_o, n_d, nat, stage,
                        dkey, waits=()):
    No = n_o * 128
    Dw = n_d * 128
    tl = None
    for a in range(n_o):
        src = bass.AP(w_handle, (e_idx * No + a * 128) * Dw,
                      [[Dw, 128], [1, Dw]])
        tl = P.e("gpsimd", lambda e, a=a, src=src: e.dma_start(
            out=nat[:, a * Dw:(a + 1) * Dw], in_=src),
            waits=list(waits), sig="dma", dkey=dkey)
    si = 0
    last = None
    for kd in range(n_d):
        for a0 in range(0, n_o, 8):
            na = min(8, n_o - a0)
            tt = None
            for ai in range(na):
                a = a0 + ai
                tt = P.e("tensor", lambda e, kd=kd, a=a, ai=ai: e.transpose(
                    c.psumT[:, ai * 128:(ai + 1) * 128],
                    nat[:, a * Dw + kd * 128: a * Dw + kd * 128 + 128],
                    c.identB[:]),
                    waits=[tl, c.tok_consts, c.bank_tok.get("A")]
                    if ai == 0 else [],
                    sig="c" if ai == na - 1 else None)
            sg = stage[:, si * 1024:(si + 1) * 1024]
            ev = P.e("scalar", lambda e, sg=sg, na=na: e.copy(
                out=sg[:, 0:na * 128], in_=c.psumT[:, 0:na * 128]),
                waits=[tt, c.bank_tok.get(("wst", si))], sig="c")
            c.bank_tok["A"] = ev
            dofs = e_idx * 128 * (n_o * n_d * 128) \
                + kd * No + a0 * 128
            dst_ap = bass.AP(dst_dram, dofs,
                             [[n_o * n_d * 128, 128], [1, na * 128]])
            st = P.e("sync", lambda e, dst_ap=dst_ap, sg=sg, na=na:
                     e.dma_start(out=dst_ap, in_=sg[:, 0:na * 128]),
                     waits=[ev], sig="dma", dkey=("wst", si))
            c.bank_tok[("wst", si)] = st
            si = 1 - si
            last = st
    return last, tl


def stg32(c):
    return c.tmp2k[:, :]


def emit_l3_ffn(P, c, x_src, x_dst):
    nc = P.nc
    ins = c.ins
    A32, A16 = c.A32, c.A16
    aw = c.arena_war
    w1T_d = _dram(nc, "w1T_d", (8, 128, 32 * 1024), BF16)
    w2T_d = _dram(nc, "w2T_d", (8, 128, 32 * 1024), BF16)
    acc_d = _dram(nc, "acc_d", (TOK, D), F32)

    A32.push()
    A16.push()
    A16.push()
    nat = A16.take(32 * 1024)
    stage = A16.take(2048)
    tprev = [aw]
    for e_i in range(8):
        t1, l1 = transpose_w_to_dram(P, c, ins["L3_ffn_w1"], w1T_d, e_i,
                                     32, 8, nat, stage, ("wld",),
                                     waits=tprev)
        tprev = [t1]
    for e_i in range(8):
        t1, l1 = transpose_w_to_dram(P, c, ins["L3_ffn_w2"], w2T_d, e_i,
                                     8, 32, nat, stage, ("wld",),
                                     waits=tprev)
        tprev = [t1]
    A16.pop()
    tw_done_toks = [(P.dma_sem(("wst", i)), P.cnt[P.dma_sem(("wst", i))], "#")
                    for i in (0, 1)]
    tw_done = tprev[0]

    gwT = A16.take(64)
    A16.push()
    gnat = A16.take(1024)
    tg = P.e("gpsimd", lambda e: e.dma_start(
        out=gnat[0:8, :], in_=bass.AP(ins["L3_ffn_gate_w"], 0,
                                      [[1024, 8], [1, 1024]])),
        waits=tprev, sig="dma", dkey=("gw",))
    tgw = None
    for kd in range(8):
        tt = P.e("tensor", lambda e, kd=kd: e.transpose(
            c.psumT[:, 0:8], gnat[0:8, kd * 128:(kd + 1) * 128],
            c.identB[0:8, 0:8]),
            waits=[tg, c.tok_consts, c.bank_tok.get("A")], sig="c")
        ev = P.e("scalar", lambda e, kd=kd: e.copy(
            out=gwT[:, kd * 8:(kd + 1) * 8], in_=c.psumT[:, 0:8]),
            waits=[tt], sig="c")
        c.bank_tok["A"] = ev
        tgw = ev
    A16.pop()
    gnat_free = tgw

    xin = A32.take(2048)
    yts = A32.take(2048)
    yns = A32.take(2048)
    comb = A32.take(64)
    lgt = A32.take(32)
    xhb = A16.take(2048)
    xhat = A16.take(8 * 1024)
    hfm = A16.take(32 * 1024)
    w2h = A16.take(16 * 1024)
    w1b = A16.take(2 * 1024)

    prev_xuse = [None, None]
    prev_store = [None, None]
    acc_tok = {}
    prev_huse = None
    prev_xhat_use = [tw_done]
    comb_war = None
    for b in range(B):
        for tb in range(2):
            tb_g = b * 2 + tb
            ev_all = []
            gate_toks = []
            for it8 in range(8):
                it = tb * 8 + it8
                i = b * 16 + it
                sl = it % 2
                xi = xin[:, sl * 1024:(sl + 1) * 1024]
                xbs = xhb[:, sl * 1024:(sl + 1) * 1024]
                ti = P.e("sync", lambda e, i=i, xi=xi: e.dma_start(
                    out=xi, in_=x_src.ap()[i * 128:(i + 1) * 128, :]),
                    waits=[prev_xuse[sl]], sig="dma", dkey=("xin", sl))
                tln = ln_tile(P, c, xi, xbs,
                              [ti, gnat_free]
                              + (prev_xhat_use if it8 < 2 else []),
                              slot=sl)
                prev_xuse[sl] = tln
                for half in range(2):
                    poff = 1024 if half == 0 else 0
                    bkey = "B" if half == 0 else "A"
                    tt = None
                    for j in range(4):
                        jj = half * 4 + j
                        tt = P.e("tensor", lambda e, jj=jj, j=j, poff=poff,
                                 xbs=xbs: e.transpose(
                                     c.psumT[:, poff + j * 128:
                                             poff + (j + 1) * 128],
                                     xbs[:, jj * 128:(jj + 1) * 128],
                                     c.identB[:]),
                                 waits=([tln, c.tok_consts,
                                         c.bank_tok.get(bkey)]
                                        + (prev_xhat_use if it8 < 2 else []))
                                 if j == 0 else [],
                                 sig="c" if j == 3 else None)
                    dsth = bass.AP(xhat.tensor,
                                   xhat.offset + (half * 4) * 1024
                                   + it8 * 128,
                                   [[A16.n, 128], [1024, 4], [1, 128]])
                    ev = P.e("scalar", lambda e, poff=poff, dsth=dsth: e.copy(
                        out=dsth, in_=c.psumT[:, poff:poff + 512]),
                        waits=[tt], sig="c")
                    c.bank_tok[bkey] = ev
                    ev_all.append(ev)
                gmm = None
                for kd in range(8):
                    gmm = P.e("tensor", lambda e, kd=kd, it8=it8: e.matmul(
                        c.psum[3][:, 0:8],
                        xhat[:, kd * 1024 + it8 * 128:
                             kd * 1024 + it8 * 128 + 128],
                        gwT[:, kd * 8:(kd + 1) * 8],
                        start=(kd == 0), stop=(kd == 7)),
                        waits=[tgw, ev_all[-1], ev_all[-2],
                               c.bank_tok.get("p3")] if kd == 0 else [],
                        sig="c" if kd == 7 else None)
                tl0 = P.e("vector", lambda e: e.tensor_copy(
                    out=lgt[:, 0:8], in_=c.psum[3][:, 0:8]),
                    waits=[gmm, comb_war], sig="c")
                c.bank_tok["p3"] = tl0
                t1 = P.e("vector", lambda e: e.reduce_max(
                    out=lgt[:, 8:9], in_=lgt[:, 0:8], axis=AX),
                    waits=[tl0], sig="c")
                t2 = P.e("vector", lambda e: e.tensor_scalar(
                    out=lgt[:, 16:24], in0=lgt[:, 0:8], scalar1=lgt[:, 8:9],
                    scalar2=None, op0=ALU.is_equal), waits=[t1], sig="c")
                t3 = P.e("vector", lambda e: e.scalar_tensor_tensor(
                    out=lgt[:, 24:32], in0=lgt[:, 16:24], scalar=-1e30,
                    in1=lgt[:, 0:8], op0=ALU.mult, op1=ALU.add),
                    waits=[t2], sig="c")
                t4 = P.e("vector", lambda e: e.reduce_max(
                    out=lgt[:, 9:10], in_=lgt[:, 24:32], axis=AX),
                    waits=[t3], sig="c")
                t5 = P.e("vector", lambda e: e.tensor_scalar(
                    out=lgt[:, 10:11], in0=lgt[:, 8:9], scalar1=lgt[:, 9:10],
                    scalar2=None, op0=ALU.subtract), waits=[t4], sig="c")
                t6 = P.e("scalar", lambda e: e.activation(
                    out=lgt[:, 11:12], in_=lgt[:, 10:11], func=AF.Sigmoid),
                    waits=[t5], sig="c")
                t7 = P.e("vector", lambda e: e.tensor_scalar(
                    out=lgt[:, 12:13], in0=lgt[:, 11:12], scalar1=-1.0,
                    scalar2=1.0, op0=ALU.mult, op1=ALU.add),
                    waits=[t6], sig="c")
                t8 = P.e("vector", lambda e: e.tensor_scalar(
                    out=lgt[:, 24:32], in0=lgt[:, 24:32], scalar1=lgt[:, 9:10],
                    scalar2=None, op0=ALU.is_equal), waits=[t7], sig="c")
                t9 = P.e("vector", lambda e: e.tensor_scalar(
                    out=lgt[:, 16:24], in0=lgt[:, 16:24], scalar1=lgt[:, 11:12],
                    scalar2=None, op0=ALU.mult), waits=[t8], sig="c")
                t10 = P.e("vector", lambda e: e.tensor_scalar(
                    out=lgt[:, 24:32], in0=lgt[:, 24:32], scalar1=lgt[:, 12:13],
                    scalar2=None, op0=ALU.mult), waits=[t9], sig="c")
                gtok = P.e("vector", lambda e, it8=it8: e.tensor_tensor(
                    out=comb[:, it8 * 8:(it8 + 1) * 8], in0=lgt[:, 16:24],
                    in1=lgt[:, 24:32], op=ALU.add), waits=[t10], sig="c")
                gate_toks.append(gtok)
            xhat_ready = ev_all[-1]

            h_last_mm = None
            for e_i in range(8):
                w1tok = [None, None]
                hev = None
                for mt in range(32):
                    sb_ = mt % 2
                    wb = w1b[:, sb_ * 1024:(sb_ + 1) * 1024]
                    src = bass.AP(w1T_d, e_i * 128 * 32768 + mt * 128,
                                  [[32768, 128], [4096, 8], [1, 128]])
                    tw1 = P.e("scalar", lambda e, src=src, wb=wb:
                              e.dma_start(out=wb, in_=src),
                              waits=[w1tok[sb_]] + tw_done_toks, sig="dma",
                              dkey=("w1l", sb_))
                    mm = None
                    for ncx in range(2):
                        pb = c.psum[2] if ncx == 0 else c.psum[5]
                        bk = "p2" if ncx == 0 else "p5"
                        mm = None
                        for kd in range(8):
                            mm = P.e("tensor", lambda e, kd=kd, ncx=ncx,
                                     pb=pb, wb=wb: e.matmul(
                                         pb[:, 0:512],
                                         wb[:, kd * 128:(kd + 1) * 128],
                                         xhat[:, kd * 1024 + ncx * 512:
                                              kd * 1024 + ncx * 512 + 512],
                                         start=(kd == 0), stop=(kd == 7)),
                                     waits=[tw1, xhat_ready,
                                            c.bank_tok.get(bk)]
                                     if kd == 0 else [],
                                     sig="c" if kd == 7 else None)
                        hev = P.e("scalar", lambda e, mt=mt, ncx=ncx, pb=pb:
                                  e.activation(
                                      out=hfm[:, mt * 1024 + ncx * 512:
                                              mt * 1024 + ncx * 512 + 512],
                                      in_=pb[:, 0:512], func=AF.Relu),
                                  waits=[mm, prev_huse], sig="c")
                        c.bank_tok[bk] = hev
                    w1tok[sb_] = mm
                h_ready = hev
                for ocx in range(2):
                    ld = None
                    for q in range(8):
                        src = bass.AP(w2T_d,
                                      e_i * 128 * 32768 + q * 4 * 1024
                                      + ocx * 512,
                                      [[32768, 128], [1024, 4], [1, 512]])
                        ld = P.e("scalar", lambda e, src=src, q=q:
                                 e.dma_start(
                                     out=w2h[:, q * 2048:(q + 1) * 2048],
                                     in_=src),
                                 waits=[c.bank_tok.get("w2h")]
                                 + tw_done_toks,
                                 sig="dma", dkey=("w2l",))
                    for st in range(8):
                        pb = c.psum[2] if st % 2 == 0 else c.psum[5]
                        bk = "p2" if st % 2 == 0 else "p5"
                        mm = None
                        for kf in range(32):
                            mm = P.e("tensor", lambda e, kf=kf, st=st, pb=pb:
                                     e.matmul(
                                         pb[:, 0:512],
                                         hfm[:, kf * 1024 + st * 128:
                                             kf * 1024 + st * 128 + 128],
                                         w2h[:, kf * 512:(kf + 1) * 512],
                                         start=(kf == 0), stop=(kf == 31)),
                                     waits=[ld, h_ready,
                                            c.bank_tok.get(bk)]
                                     if kf == 0 else [],
                                     sig="c" if kf == 31 else None)
                        h_last_mm = mm
                        slot = (st % 2) * 2 + ocx
                        ev = P.e("vector", lambda e, st=st, e_i=e_i, pb=pb,
                                 slot=slot: e.tensor_scalar(
                                     out=stg32(c)[:, slot * 512:
                                                  slot * 512 + 512],
                                     in0=pb[:, 0:512],
                                     scalar1=comb[:, st * 8 + e_i:
                                                  st * 8 + e_i + 1],
                                     scalar2=None, op0=ALU.mult),
                                 waits=[mm, gate_toks[st],
                                        acc_tok.get((st % 2, ocx))],
                                 sig="c")
                        c.bank_tok[bk] = ev
                        dst = acc_d.ap()[tb_g * 1024 + st * 128:
                                         tb_g * 1024 + (st + 1) * 128,
                                         ocx * 512:(ocx + 1) * 512]
                        if e_i == 0:
                            adm = P.e("gpsimd", lambda e, dst=dst,
                                      slot=slot: e.dma_start(
                                          out=dst,
                                          in_=stg32(c)[:, slot * 512:
                                                       slot * 512 + 512]),
                                      waits=[ev], sig="dma",
                                      dkey=("acc", st % 2, ocx))
                        else:
                            adm = P.e("gpsimd", lambda e, dst=dst,
                                      slot=slot: e.dma_start(
                                          out=dst,
                                          in_=stg32(c)[:, slot * 512:
                                                       slot * 512 + 512],
                                          accum_op=ALU.add),
                                      waits=[ev], sig="dma",
                                      dkey=("acc", st % 2, ocx))
                        acc_tok[(st % 2, ocx)] = adm
                    c.bank_tok["w2h"] = h_last_mm
                prev_huse = h_last_mm
            prev_xhat_use = [h_last_mm]
            comb_war = None

            for it8 in range(8):
                it = tb * 8 + it8
                i = b * 16 + it
                sl = it % 2
                xi = xin[:, sl * 1024:(sl + 1) * 1024]
                ya = yts[:, sl * 1024:(sl + 1) * 1024]
                yb = yns[:, sl * 1024:(sl + 1) * 1024]
                tl_ = P.e("sync", lambda e, i=i, ya=ya: e.dma_start(
                    out=ya, in_=acc_d.ap()[i * 128:(i + 1) * 128, :]),
                    waits=[prev_store[sl], acc_tok.get((it8 % 2, 0)),
                           acc_tok.get((it8 % 2, 1))],
                    sig="dma", dkey=("accr", sl))
                txi = P.e("sync", lambda e, i=i, xi=xi: e.dma_start(
                    out=xi, in_=x_src.ap()[i * 128:(i + 1) * 128, :]),
                    waits=[prev_xuse[sl]], sig="dma", dkey=("xin", sl))
                t3 = ln_tile(P, c, ya, yb, [tl_, prev_store[sl]], slot=sl)
                t4 = P.e("vector", lambda e, yb=yb, xi=xi: e.tensor_tensor(
                    out=yb, in0=yb, in1=xi, op=ALU.add),
                    waits=[t3, txi], sig="c")
                prev_xuse[sl] = t4
                stt = P.e("sync", lambda e, i=i, yb=yb: e.dma_start(
                    out=x_dst.ap()[i * 128:(i + 1) * 128, :], in_=yb),
                    waits=[t4], sig="dma", dkey=("xout", sl))
                prev_store[sl] = stt
            comb_war = gate_toks[-1]
    A32.pop()
    A16.pop()
    P.barrier()
    c.arena_war = None




def emit_l2_attn(P, c, x_src, x_dst):
    nc = P.nc
    ins = c.ins
    A32, A16 = c.A32, c.A16
    aw = c.arena_war
    qkvT_d = _dram(nc, "qkvT_d", (1, 128, 24 * 1024), BF16)
    rp_d = [_dram(nc, f"rp_d{i}", (T, 1664), BF16) for i in range(2)]

    A32.push()
    A16.push()
    A16.push()
    nat = A16.take(24 * 1024)
    stage = A16.take(2048)
    tqT, _l = transpose_w_to_dram(P, c, ins["L2_attn_Wqkv"], qkvT_d, 0,
                                  24, 8, nat, stage, ("wld",), waits=[aw])
    A16.pop()
    tw_done_toks = [(P.dma_sem(("wst", i)), P.cnt[P.dma_sem(("wst", i))], "#")
                    for i in (0, 1)]

    wrT = A16.take(272)
    brb = A32.take(272)
    bqb = A32.take(32)
    bvb = A32.take(1024)
    A16.push()
    wnat = A16.take(192)
    twr = P.e("gpsimd", lambda e: e.dma_start(
        out=wnat[:, 0:64], in_=bass.AP(ins["L2_attn_Wr"], 0,
                                       [[64, 128], [1, 64]])),
        waits=[aw] + tw_done_toks, sig="dma", dkey=("wr",))
    twr = P.e("gpsimd", lambda e: e.dma_start(
        out=wnat[:, 64:128], in_=bass.AP(ins["L2_attn_Wr"], 128 * 64,
                                         [[64, 128], [1, 64]])),
        waits=[aw], sig="dma", dkey=("wr",))
    twr = P.e("gpsimd", lambda e: e.dma_start(
        out=wrT[0:64, 256:257], in_=bass.AP(ins["L2_attn_Wr"], 256 * 64,
                                            [[1, 64], [1, 1]])),
        waits=[aw] + tw_done_toks, sig="dma", dkey=("wr",))
    twr = P.e("gpsimd", lambda e: e.dma_start(
        out=wrT[64:128, 256:257], in_=bass.AP(ins["L2_attn_Wr"], 256 * 64,
                                              [[1, 64], [1, 1]])),
        waits=[aw], sig="dma", dkey=("wr",))
    tl256 = twr
    tts = []
    for half in range(2):
        tt = P.e("tensor", lambda e, half=half: e.transpose(
            c.psumT[0:64, half * 128:(half + 1) * 128],
            wnat[:, half * 64:(half + 1) * 64], c.identB[:]),
            waits=[twr, c.tok_consts, c.bank_tok.get("A")], sig="c")
        tt2 = P.e("tensor", lambda e, half=half: e.transpose(
            c.psumT[64:128, 256 + half * 128: 256 + (half + 1) * 128],
            wnat[:, half * 64:(half + 1) * 64], c.identB[:]),
            waits=[], sig="c")
        tts += [tt, tt2]
    evw = P.e("scalar", lambda e: e.copy(out=wrT[0:64, 0:256],
                                         in_=c.psumT[0:64, 0:256]),
              waits=tts, sig="c")
    evw = P.e("scalar", lambda e: e.copy(out=wrT[64:128, 0:256],
                                         in_=c.psumT[64:128, 256:512]),
              waits=tts, sig="c")
    c.bank_tok["A"] = evw
    tbr = P.e("sync", lambda e: e.dma_start(
        out=brb[:, 0:257], in_=bass.AP(ins["L2_attn_br"], 0,
                                       [[0, 128], [1, 257]])),
        waits=[aw], sig="dma", dkey=("l2b",))
    tbq = P.e("sync", lambda e: e.dma_start(
        out=bqb[:, 0:16], in_=bass.AP(ins["L2_attn_bqkv"], 0,
                                      [[1, 128], [192, 16], [1, 1]]),
        allow_slow_non_contiguous=True), waits=[aw], sig="dma",
        dkey=("l2b",))
    tbq = P.e("sync", lambda e: e.dma_start(
        out=bqb[:, 16:32], in_=bass.AP(ins["L2_attn_bqkv"], 64,
                                       [[1, 128], [192, 16], [1, 1]]),
        allow_slow_non_contiguous=True), waits=[aw], sig="dma",
        dkey=("l2b",))
    tbv = P.e("sync", lambda e: e.dma_start(
        out=bvb, in_=bass.AP(ins["L2_attn_bqkv"], 128,
                             [[0, 128], [192, 16], [1, 64]])),
        waits=[aw], sig="dma", dkey=("l2b",))
    tl2b = tbv
    tbq = tl2b
    tbv = tl2b
    tbr = P.e("vector", lambda e: e.tensor_scalar(
        out=brb[:, 0:257], in0=brb[:, 0:257], scalar1=8.0, scalar2=None,
        op0=ALU.mult), waits=[tl2b], sig="c")
    A16.pop()
    wnat_free = evw

    xin = A32.take(2048)
    yts = A32.take(1024)
    yns = A32.take(1024)
    xep = A32.take(2048)
    Sf = A32.take(2048)
    rl = A32.take(8)
    redg = A32.take(64)
    xhb = A16.take(2048)
    xhat = A16.take(16384)
    q2 = A16.take(2048)
    k2 = A16.take(2048)
    vth = A16.take(4096)
    wqh = A16.take(2 * 3072)
    rpad = A16.take(1664)
    Pb = A16.take(2048)
    PT = A16.take(2048)
    ofm = A16.take(128)
    ytm = A16.take(16384)

    prev_xuse = [None, None]
    prev_store = [None, None]
    prev_epi = [wnat_free]
    wqh_tok = [None, None]
    rp_read_tok = [None, None]
    rp_fill_tok = [None, None]
    for b in range(B):
        ev_all = []
        for it in range(16):
            i = b * 16 + it
            sl = it % 2
            xi = xin[:, sl * 1024:(sl + 1) * 1024]
            xbs = xhb[:, sl * 1024:(sl + 1) * 1024]
            ti = P.e("sync", lambda e, i=i, xi=xi: e.dma_start(
                out=xi, in_=x_src.ap()[i * 128:(i + 1) * 128, :]),
                waits=[prev_xuse[sl]],
                sig="dma", dkey=("xin", sl))
            tln = ln_tile(P, c, xi, xbs,
                          [ti] + (prev_epi if it < 2 else []), slot=sl)
            prev_xuse[sl] = tln
            for half in range(2):
                poff = 1024 if half == 0 else 0
                bkey = "B" if half == 0 else "A"
                tt = None
                for j in range(4):
                    jj = half * 4 + j
                    tt = P.e("tensor", lambda e, jj=jj, j=j, poff=poff,
                             xbs=xbs: e.transpose(
                                 c.psumT[:, poff + j * 128:
                                         poff + (j + 1) * 128],
                                 xbs[:, jj * 128:(jj + 1) * 128],
                                 c.identB[:]),
                             waits=([tln, c.tok_consts, c.bank_tok.get(bkey)]
                                    + (prev_epi if it < 2 else []))
                             if j == 0 else [],
                             sig="c" if j == 3 else None)
                dsth = bass.AP(xhat.tensor,
                               xhat.offset + (half * 4) * 2048 + it * 128,
                               [[A16.n, 128], [2048, 4], [1, 128]])
                ev = P.e("scalar", lambda e, poff=poff, dsth=dsth: e.copy(
                    out=dsth, in_=c.psumT[:, poff:poff + 512]),
                    waits=[tt], sig="c")
                c.bank_tok[bkey] = ev
                ev_all.append(ev)
        xhat_ready = ev_all[-1]

        for hpair in range(8):
            hp = hpair % 2
            rp_buf = rp_d[hp]
            h0 = 2 * hpair
            tw = None
            for half in range(2):
                tw = P.e("scalar", lambda e, h0=h0, hp=hp, half=half:
                         e.dma_start(
                             out=wqh[:, hp * 3072 + half * 1536:
                                     hp * 3072 + (half + 1) * 1536],
                             in_=bass.AP(qkvT_d, half * 4 * 3072 + h0 * 192,
                                         [[24576, 128], [3072, 4],
                                          [1, 384]])),
                         waits=[wqh_tok[hp]] + tw_done_toks, sig="dma",
                         dkey=("wqh", hp))
            qk_last = None
            for which, dstb, ofs in (("q", q2, 0), ("k", k2, 64)):
                for ncx in range(4):
                    pb = c.psum[2] if ncx % 2 == 0 else c.psum[5]
                    bk = "p2" if ncx % 2 == 0 else "p5"
                    mm = None
                    for sub in range(2):
                        for kd in range(8):
                            mm = P.e("tensor", lambda e, kd=kd, ncx=ncx,
                                     pb=pb, hp=hp, sub=sub, ofs=ofs:
                                     e.matmul(
                                         pb[sub * 64:sub * 64 + 64, 0:512],
                                         wqh[:, hp * 3072 + kd * 384
                                             + sub * 192 + ofs:
                                             hp * 3072 + kd * 384
                                             + sub * 192 + ofs + 64],
                                         xhat[:, kd * 2048 + ncx * 512:
                                              kd * 2048 + ncx * 512 + 512],
                                         start=(kd == 0), stop=(kd == 7)),
                                     waits=[tw, xhat_ready,
                                            c.bank_tok.get(bk)]
                                     if kd == 0 and sub == 0 else [],
                                     sig="c" if kd == 7 and sub == 1
                                     else None)
                    ev = None
                    for sub in range(2):
                        bcol = (h0 + sub) + (0 if which == "q" else 16)
                        ev = P.e("vector", lambda e, ncx=ncx, pb=pb,
                                 sub=sub, bcol=bcol, dstb=dstb:
                                 e.tensor_scalar(
                                     out=dstb[sub * 64:sub * 64 + 64,
                                              ncx * 512:(ncx + 1) * 512],
                                     in0=pb[sub * 64:sub * 64 + 64, 0:512],
                                     scalar1=bqb[sub * 64:sub * 64 + 64,
                                                 bcol:bcol + 1],
                                     scalar2=None, op0=ALU.add),
                                 waits=[mm, tbq]
                                 + (prev_epi if ncx == 0 else []),
                                 sig="c")
                    c.bank_tok[bk] = ev
                    qk_last = ev
            v_done = None
            for tt4 in range(16):
                for sub in range(2):
                    gidx = tt4 * 2 + sub
                    pbk = c.psum[2] if gidx % 2 == 0 else c.psum[5]
                    bk = "p2" if gidx % 2 == 0 else "p5"
                    mm = None
                    for kd in range(8):
                        mm = P.e("tensor", lambda e, kd=kd, tt4=tt4,
                                 pbk=pbk, hp=hp, sub=sub: e.matmul(
                                     pbk[:, 0:64],
                                     xhat[:, kd * 2048 + tt4 * 128:
                                          kd * 2048 + tt4 * 128 + 128],
                                     wqh[:, hp * 3072 + kd * 384
                                         + sub * 192 + 128:
                                         hp * 3072 + kd * 384
                                         + sub * 192 + 192],
                                     start=(kd == 0), stop=(kd == 7)),
                                 waits=[tw, xhat_ready, c.bank_tok.get(bk)]
                                 if kd == 0 else [],
                                 sig="c" if kd == 7 else None)
                    ev = P.e("vector", lambda e, tt4=tt4, pbk=pbk, h0=h0,
                             sub=sub: e.tensor_tensor(
                                 out=vth[:, tt4 * 128 + sub * 64:
                                         tt4 * 128 + sub * 64 + 64],
                                 in0=pbk[:, 0:64],
                                 in1=bvb[:, (h0 + sub) * 64:
                                         (h0 + sub) * 64 + 64], op=ALU.add),
                             waits=[mm, tbv], sig="c")
                    c.bank_tok[bk] = ev
                    v_done = ev
            qkv_done = qk_last

            for sub in range(2):
                h = h0 + sub
                base = sub * 64
                fill_tok = rp_fill_tok[hp]
                redg_tok = None
                for it in range(16):
                    mm = P.e("tensor", lambda e, it=it, base=base: e.matmul(
                        c.psum[3][:, 0:257],
                        q2[base:base + 64, it * 128:(it + 1) * 128],
                        wrT[base:base + 64, 0:257], start=True, stop=True),
                        waits=[qkv_done, evw, tl256, v_done,
                               c.bank_tok.get("p3")], sig="c")
                    evr = P.e("vector", lambda e: e.scalar_tensor_tensor(
                        out=rpad[:, 640:897], in0=c.psum[3][:, 0:257],
                        scalar=8.0, in1=brb[:, 0:257], op0=ALU.mult,
                        op1=ALU.add),
                        waits=[mm, tbr, rp_read_tok[hp], fill_tok], sig="c")
                    c.bank_tok["p3"] = evr
                    e0 = P.e("vector", lambda e, it=it: e.tensor_copy(
                        out=redg[:, 2 * it:2 * it + 1], in_=rpad[:, 640:641]),
                        waits=[evr], sig="c")
                    e1 = P.e("vector", lambda e, it=it: e.tensor_copy(
                        out=redg[:, 2 * it + 1:2 * it + 2],
                        in_=rpad[:, 896:897]),
                        waits=[evr], sig="c")
                    tp1 = P.e("vector", lambda e, it=it: e.tensor_scalar(
                        out=rpad[:, 0:640], in0=c.ones2k[:, 0:640],
                        scalar1=redg[:, 2 * it:2 * it + 1], scalar2=None,
                        op0=ALU.mult), waits=[e0, e1], sig="c")
                    tp2 = P.e("vector", lambda e, it=it: e.tensor_scalar(
                        out=rpad[:, 897:1664], in0=c.ones2k[:, 0:767],
                        scalar1=redg[:, 2 * it + 1:2 * it + 2], scalar2=None,
                        op0=ALU.mult), waits=[tp1], sig="c")
                    fill_tok = P.e("sync", lambda e, it=it, rp_buf=rp_buf:
                                   e.dma_start(
                                       out=bass.AP(rp_buf, it * 128 * 1664,
                                                   [[1664, 128], [1, 1664]]),
                                       in_=rpad[:, 0:1664]),
                                   waits=[tp2], sig="dma", dkey=("rpf", hp))
                    redg_tok = e1
                rp_fill_tok[hp] = fill_tok

                last_bias_read = None
                prev_texp = None
                mmo = None
                for qt in range(16):
                    i0 = qt * 128
                    sev = []
                    for jb in range(4):
                        j0 = jb * 512
                        pb = c.psum[2] if jb % 2 == 0 else c.psum[5]
                        bk = "p2" if jb % 2 == 0 else "p5"
                        near = (i0 - 639 <= j0 <= i0 + 255)
                        mm = P.e("tensor", lambda e, qt=qt, jb=jb, pb=pb,
                                 base=base: e.matmul(
                                     pb[:, 0:512],
                                     q2[base:base + 64,
                                        qt * 128:(qt + 1) * 128],
                                     k2[base:base + 64,
                                        jb * 512:(jb + 1) * 512],
                                     start=True, stop=True),
                                 waits=[qkv_done, c.bank_tok.get(bk)],
                                 sig="c")
                        _pt = prev_texp
                        if near:
                            bofs = i0 * 1664 + (j0 - i0 + 768)
                            tpr = P.e("scalar", lambda e, bofs=bofs,
                                      rp_buf=rp_buf: e.dma_start(
                                          out=PT[:, 0:512],
                                          in_=bass.AP(rp_buf, bofs,
                                                      [[1663, 128],
                                                       [1, 512]])),
                                      waits=[fill_tok,
                                             c.bank_tok.get("PTpos")],
                                      sig="dma", dkey=("rpr", hp))
                            ev = P.e("vector", lambda e, jb=jb, pb=pb:
                                     e.tensor_tensor(
                                         out=Sf[:, jb * 512:(jb + 1) * 512],
                                         in0=pb[:, 0:512], in1=PT[:, 0:512],
                                         op=ALU.add),
                                     waits=[mm, tpr, _pt], sig="c")
                            c.bank_tok["PTpos"] = ev
                            last_bias_read = tpr
                        else:
                            eidx = 2 * qt + (0 if j0 < i0 else 1)
                            ev = P.e("vector", lambda e, jb=jb, pb=pb,
                                     eidx=eidx: e.tensor_scalar(
                                         out=Sf[:, jb * 512:(jb + 1) * 512],
                                         in0=pb[:, 0:512],
                                         scalar1=redg[:, eidx:eidx + 1],
                                         scalar2=None, op0=ALU.add),
                                     waits=[mm, redg_tok, _pt], sig="c")
                        c.bank_tok[bk] = ev
                        sev.append(ev)
                    tnm = P.e("vector", lambda e: e.reduce_max(
                        out=rl[:, 0:1], in_=Sf, axis=AX, negate=True),
                        waits=sev, sig="c")
                    tnm8 = P.e("vector", lambda e: e.tensor_scalar(
                        out=rl[:, 1:2], in0=rl[:, 0:1], scalar1=0.125,
                        scalar2=None, op0=ALU.mult), waits=[tnm], sig="c")
                    texp = P.e("scalar", lambda e: e.activation(
                        out=Pb, in_=Sf, func=AF.Exp, scale=0.125,
                        bias=rl[:, 1:2], accum_out=rl[:, 2:3]),
                        waits=[tnm8] + sev, sig="c")
                    prev_texp = texp
                    trl = P.e("vector", lambda e: e.reciprocal(
                        out=rl[:, 3:4], in_=rl[:, 2:3]),
                        waits=[texp], sig="c")
                    for g4 in range(4):
                        poff = 1024 if g4 % 2 == 0 else 0
                        bkey = "B" if g4 % 2 == 0 else "A"
                        tt = None
                        for j in range(4):
                            kt = g4 * 4 + j
                            tt = P.e("tensor", lambda e, kt=kt, j=j,
                                     poff=poff: e.transpose(
                                         c.psumT[:, poff + j * 128:
                                                 poff + (j + 1) * 128],
                                         Pb[:, kt * 128:(kt + 1) * 128],
                                         c.identB[:]),
                                     waits=[texp, c.tok_consts,
                                            c.bank_tok.get(bkey)]
                                     if j == 0 else [],
                                     sig="c" if j == 3 else None)
                        evp = P.e("scalar", lambda e, g4=g4, poff=poff:
                                  e.copy(
                                      out=PT[:, g4 * 512:(g4 + 1) * 512],
                                      in_=c.psumT[:, poff:poff + 512]),
                                  waits=[tt, c.bank_tok.get("PTpos")],
                                  sig="c")
                        c.bank_tok[bkey] = evp
                        c.bank_tok["PTpos"] = evp
                        for j in range(4):
                            kt = g4 * 4 + j
                            mmo = P.e("tensor", lambda e, kt=kt, sub=sub:
                                      e.matmul(
                                          c.psum[4][0:64, 0:128],
                                          vth[:, kt * 128 + sub * 64:
                                              kt * 128 + sub * 64 + 64],
                                          PT[:, kt * 128:(kt + 1) * 128],
                                          start=(kt == 0), stop=(kt == 15)),
                                      waits=[evp, v_done,
                                             c.bank_tok.get("p4")]
                                      if j == 0 else [],
                                      sig="c" if kt == 15 else None)
                    evo = P.e("vector", lambda e: e.tensor_copy(
                        out=ofm[0:64, 0:128], in_=c.psum[4][0:64, 0:128]),
                        waits=[mmo], sig="c")
                    c.bank_tok["p4"] = evo
                    tt = P.e("tensor", lambda e: e.transpose(
                        c.psumT[0:128, 1024:1088], ofm[0:64, 0:128],
                        c.identB[0:64, 0:64]),
                        waits=[evo, c.tok_consts, c.bank_tok.get("B")],
                        sig="c")
                    evy = P.e("vector", lambda e, qt=qt, h=h:
                              e.tensor_scalar(
                                  out=bass.AP(ytm.tensor,
                                              ytm.offset + qt * 1024
                                              + h * 64,
                                              [[A16.n, 128], [1, 64]]),
                                  in0=c.psumT[0:128, 1024:1088],
                                  scalar1=rl[:, 3:4], scalar2=None,
                                  op0=ALU.mult),
                              waits=[tt, trl], sig="c")
                    c.bank_tok["B"] = evy
                rp_read_tok[hp] = last_bias_read
            wqh_tok[hp] = mmo

        yready = c.bank_tok["B"]
        for it in range(16):
            i = b * 16 + it
            sl = it % 2
            xi = xep[:, sl * 1024:(sl + 1) * 1024]
            txi = P.e("sync", lambda e, i=i, xi=xi: e.dma_start(
                out=xi, in_=x_src.ap()[i * 128:(i + 1) * 128, :]),
                waits=[prev_store[sl]], sig="dma", dkey=("xep", sl))
            tcp = P.e("vector", lambda e, it=it: e.tensor_copy(
                out=yts, in_=bass.AP(ytm.tensor, ytm.offset + it * 1024,
                                     [[A16.n, 128], [1, 1024]])),
                waits=[yready, prev_store[sl]], sig="c")
            t3 = ln_tile(P, c, yts, yns,
                         [tcp, prev_store[0], prev_store[1]], slot=sl)
            t4 = P.e("vector", lambda e, xi=xi: e.tensor_tensor(
                out=yns, in0=yns, in1=xi, op=ALU.add),
                waits=[t3, txi], sig="c")
            stt = P.e("sync", lambda e, i=i: e.dma_start(
                out=x_dst.ap()[i * 128:(i + 1) * 128, :], in_=yns),
                waits=[t4], sig="dma", dkey=("xout", sl))
            prev_store[sl] = stt
        prev_epi = [P.prev_c["vector"]]
    A32.pop()
    A16.pop()
    P.barrier()
    c.arena_war = None





def emit_l0_attn(P, c, x_src, x_dst):
    nc = P.nc
    ins = c.ins
    A32, A16 = c.A32, c.A16
    aw = c.arena_war
    C1, tc1 = constap(P, c, 1.0)
    bias_d = _dram(nc, "fbias_d", (16, T, T), BF16)
    nd_d = _dram(nc, "nd_d", (T, T), BF16)
    q_d = _dram(nc, "q_d", (8, 128, T), BF16)
    k_d = _dram(nc, "k_d", (8, 128, T), BF16)
    v_d = _dram(nc, "v_d", (T, 1024), BF16)

    A32.push()
    A16.push()
    fsc = A32.take(32)
    G128 = A32.take(16)
    A16.push()
    w2p = A16.take(32)
    Ej = A16.take(32 * 128)

    t1 = None
    for tl in range(4):
        t1 = P.e("sync", lambda e, tl=tl: e.dma_start(
            out=bass.AP(fsc.tensor, fsc.offset + (32 * tl) * A32.n,
                        [[A32.n, 32], [1, 1]]),
            in_=bass.AP(ins["L0_attn_w1"], 0, [[1, 32], [1, 1]])),
            waits=[aw], sig="dma", dkey=("f1",))
        t1 = P.e("sync", lambda e, tl=tl: e.dma_start(
            out=bass.AP(fsc.tensor, fsc.offset + 1 + (32 * tl) * A32.n,
                        [[A32.n, 32], [1, 1]]),
            in_=bass.AP(ins["L0_attn_b1"], 0, [[1, 32], [1, 1]])),
            waits=[aw], sig="dma", dkey=("f1",))
    t1 = P.e("sync", lambda e: e.dma_start(
        out=fsc[:, 2:3], in_=bass.AP(ins["L0_attn_c_param"], 0,
                                     [[0, 128], [1, 1]])),
        waits=[aw], sig="dma", dkey=("f1",))
    t1 = P.e("sync", lambda e: e.dma_start(
        out=fsc[:, 5:6], in_=bass.AP(ins["L0_attn_L_multiplier"], 0,
                                     [[0, 128], [1, 1]])),
        waits=[aw], sig="dma", dkey=("f1",))
    tb2 = P.e("gpsimd", lambda e: e.memset(w2p[:], 0.0), waits=[aw], sig="c")
    for tl in range(4):
        tb2 = P.e("gpsimd", lambda e, tl=tl: e.dma_start(
            out=bass.AP(w2p.tensor, w2p.offset + (32 * tl) * A16.n,
                        [[A16.n, 32], [1, 16]]),
            in_=ins["L0_attn_w2"].ap()[0:32, 0:16]),
            waits=[P.prev_c["gpsimd"]], sig="dma", dkey=("f2",))
    tz = P.e("vector", lambda e: e.memset(fsc[:, 4:5], 0.0),
             waits=[aw], sig="c")
    tb2b = tz
    for tl in range(4):
        tb2b = P.e("sync", lambda e, tl=tl: e.dma_start(
            out=bass.AP(fsc.tensor, fsc.offset + 4 + (32 * tl) * A32.n,
                        [[A32.n, 16], [1, 1]]),
            in_=bass.AP(ins["L0_attn_b2"], 0, [[1, 16], [1, 1]])),
            waits=[tz, t1], sig="dma", dkey=("f1",))
    t8 = P.e("vector", lambda e: e.tensor_scalar(
        out=fsc[:, 4:5], in0=fsc[:, 4:5], scalar1=8.0, scalar2=None,
        op0=ALU.mult), waits=[tb2b, tz], sig="c")
    tth = P.e("scalar", lambda e: e.activation(
        out=fsc[:, 3:4], in_=fsc[:, 5:6], func=AF.Abs, scale=512.0),
        waits=[t1], sig="c")
    tca = P.e("scalar", lambda e: e.activation(
        out=fsc[:, 2:3], in_=fsc[:, 2:3], func=AF.Abs),
        waits=[t1], sig="c")
    tg0 = P.e("gpsimd", lambda e: e.iota(
        c.tmp2k[:, 0:16], pattern=[[128, 16]], base=0, channel_multiplier=1,
        allow_small_or_imprecise_dtypes=True), waits=[aw, c.tok_consts],
        sig="c")
    tg1 = P.e("vector", lambda e: e.tensor_scalar(
        out=G128, in0=c.tmp2k[:, 0:16], scalar1=fsc[:, 3:4], scalar2=0.0,
        op0=ALU.subtract, op1=ALU.max), waits=[tg0, tth], sig="c")
    tg1 = P.e("vector", lambda e: e.tensor_scalar(
        out=G128, in0=G128, scalar1=fsc[:, 3:4], scalar2=None,
        op0=ALU.add), waits=[tg1], sig="c")
    tg2 = P.e("vector", lambda e: e.tensor_scalar(
        out=G128, in0=G128, scalar1=fsc[:, 2:3], scalar2=None,
        op0=ALU.mult), waits=[tg1, tca], sig="c")
    tg3 = P.e("scalar", lambda e: e.activation(
        out=G128, in_=G128, func=AF.Ln, bias=C1), waits=[tg2, tc1], sig="c")
    tg4 = P.e("vector", lambda e: e.tensor_scalar(
        out=G128, in0=G128, scalar1=1e-6, scalar2=None, op0=ALU.add),
        waits=[tg3], sig="c")
    tg5 = P.e("vector", lambda e: e.reciprocal(out=G128, in_=G128),
              waits=[tg4], sig="c")
    tej = None
    for j in range(32):
        tj = P.e("gpsimd", lambda e, j=j: e.iota(
            c.tmp2k[:, 0:128], pattern=[[1, 4], [0, 32]], base=4 * j,
            channel_multiplier=-1, allow_small_or_imprecise_dtypes=True),
            waits=[tg1 if j == 0 else tej], sig="c")
        tej = P.e("vector", lambda e, j=j: e.tensor_scalar(
            out=Ej[:, j * 128:(j + 1) * 128], in0=c.tmp2k[:, 0:128],
            scalar1=0.0, scalar2=None, op0=ALU.is_equal),
            waits=[tj], sig="c")

    A32.push()
    ndt = A32.take(2048)
    nd_done = None
    for tt_ in range(16):
        t0 = tt_ * 128
        nsb = (t0 + 128 + 511) // 512
        for sb in range(nsb):
            s0 = sb * 512
            ti = P.e("gpsimd", lambda e, t0=t0, s0=s0: e.iota(
                ndt[:, 0:512], pattern=[[-1, 512]], base=t0 - s0,
                channel_multiplier=1, allow_small_or_imprecise_dtypes=True),
                waits=[nd_done], sig="c")
            tb = P.e("scalar", lambda e: e.activation(
                out=ndt[:, 512:1024], in_=ndt[:, 0:512], func=AF.Abs),
                waits=[ti], sig="c")
            tb = P.e("vector", lambda e: e.tensor_scalar(
                out=ndt[:, 512:1024], in0=ndt[:, 512:1024],
                scalar1=fsc[:, 2:3], scalar2=None, op0=ALU.mult),
                waits=[tb, tca], sig="c")
            tb = P.e("scalar", lambda e: e.activation(
                out=ndt[:, 512:1024], in_=ndt[:, 512:1024], func=AF.Ln,
                bias=C1), waits=[tb, tc1], sig="c")
            tb = P.e("vector", lambda e, tt_=tt_: e.tensor_scalar(
                out=ndt[:, 1024:1536], in0=ndt[:, 512:1024],
                scalar1=G128[:, tt_:tt_ + 1], scalar2=None, op0=ALU.mult),
                waits=[tb, tg5], sig="c")
            nd_done = P.e("gpsimd", lambda e, t0=t0, s0=s0: e.dma_start(
                out=bass.AP(nd_d, t0 * T + s0, [[T, 128], [1, 512]]),
                in_=ndt[:, 1024:1536]), waits=[tb], sig="dma", dkey=("ndw",))
    A32.pop()

    A16.push()
    ndtile = A16.take(2048)
    hbuf = A16.take(512)
    bstage = A16.take(1024)
    prev_stage = [None, None, None, None]
    prev_em = None
    nd_rd = None
    for tt_ in range(16):
        t0 = tt_ * 128
        nsb = (t0 + 128 + 511) // 512
        for sb in range(nsb):
            s0 = sb * 512
            trd = P.e("scalar", lambda e, t0=t0, s0=s0: e.dma_start(
                out=ndtile[:, 0:512], in_=bass.AP(nd_d, t0 * T + s0,
                                                  [[T, 128], [1, 512]])),
                waits=[nd_done, prev_em], sig="dma", dkey=("ndr",))
            for j in range(32):
                em = P.e("tensor", lambda e, j=j: e.matmul(
                    c.psum[0][:, 0:512], Ej[:, j * 128:(j + 1) * 128],
                    ndtile[:, 0:512], start=True, stop=True),
                    waits=[trd, tej, c.bank_tok.get("p0")], sig="c")
                prev_em = em
                th = P.e("scalar", lambda e: e.activation(
                    out=hbuf, in_=c.psum[0][:, 0:512], func=AF.Relu,
                    scale=fsc[:, 0:1], bias=fsc[:, 1:2]),
                    waits=[em, t8, tca], sig="c")
                c.bank_tok["p0"] = th
                wm = None
                wms = []
                for tl in range(4):
                    wm = P.e("tensor", lambda e, tl=tl: e.matmul(
                        c.psum[1][32 * tl:32 * tl + 32, 0:512],
                        w2p[32 * tl:32 * tl + 32, 0:32],
                        hbuf[32 * tl:32 * tl + 32, :],
                        start=True, stop=True,
                        tile_position=(32 * tl, 32 * tl)),
                        waits=[th, tb2, c.bank_tok.get("p1")]
                        if tl == 0 else [],
                        sig="c" if tl == 3 else None)
                last_wm = wm
                evb = P.e("vector", lambda e: e.tensor_scalar(
                    out=bstage[:, 0:512], in0=c.psum[1][:, 0:512],
                    scalar1=8.0, scalar2=fsc[:, 4:5],
                    op0=ALU.mult, op1=ALU.add),
                    waits=[wm, t8] + prev_stage, sig="c")
                c.bank_tok["p1"] = evb
                for tl in range(4):
                    eng = "sync" if tl % 2 == 0 else "scalar"
                    stt = P.e(eng, lambda e, tl=tl, j=j, t0=t0, s0=s0:
                              e.dma_start(
                                  out=bass.AP(bias_d,
                                              (t0 + 4 * j + tl) * T + s0,
                                              [[T * T, 16], [1, 512]]),
                                  in_=bass.AP(bstage.tensor,
                                              bstage.offset
                                              + (32 * tl) * A16.n,
                                              [[A16.n, 16], [1, 512]])),
                              waits=[evb], sig="dma", dkey=("bst", tl))
                    prev_stage[tl] = stt
    bias_done_toks = [(P.dma_sem(("bst", i)), P.cnt[P.dma_sem(("bst", i))],
                       "#") for i in range(4)]
    A16.pop()
    A16.pop()
    FW = [last_wm, nd_done] + bias_done_toks

    wqb = A16.take(2048)
    bqf = A32.take(24)
    bvb = A32.take(1024)
    bpb = A32.take(1024)
    tbq = P.e("sync", lambda e: e.dma_start(
        out=bqf, in_=bass.AP(ins["L0_attn_bqkv"], 0,
                             [[1, 128], [128, 24], [1, 1]]),
        allow_slow_non_contiguous=True), waits=[aw] + FW, sig="dma",
        dkey=("l0b",))
    tbv = P.e("sync", lambda e: e.dma_start(
        out=bvb, in_=bass.AP(ins["L0_attn_bqkv"], 2048,
                             [[0, 128], [1, 1024]])),
        waits=[aw] + FW, sig="dma", dkey=("l0b",))
    tbp = P.e("sync", lambda e: e.dma_start(
        out=bpb, in_=bass.AP(ins["L0_attn_bproj"], 0,
                             [[0, 128], [1, 1024]])),
        waits=[aw] + FW, sig="dma", dkey=("l0b",))
    tbl = tbp

    xin = A32.take(2048)
    yts = A32.take(1024)
    yns = A32.take(1024)
    xep = A32.take(2048)
    Sf = A32.take(2048)
    rl = A32.take(8)
    xhb = A16.take(2048)
    xhat = A16.take(16384)
    stq = A16.take(1024)
    qpb = A16.take(2 * 2048)
    kpb = A16.take(2 * 2048)
    vpb = A16.take(2 * 2048)
    Pb = A16.take(2048)
    PT = A16.take(2048)
    yfm = A16.take(16384)
    btile = A16.take(1024)
    wpb = A16.take(2 * 4096)

    prev_xuse = [None, None]
    prev_store = [None, None]
    prev_epi = FW
    stq_tok = [None, None]
    qkv_w_tok = [None, None]
    pair_tok = [None, None]
    for b in range(B):
        ev_all = []
        for it in range(16):
            i = b * 16 + it
            sl = it % 2
            xi = xin[:, sl * 1024:(sl + 1) * 1024]
            xbs = xhb[:, sl * 1024:(sl + 1) * 1024]
            ti = P.e("sync", lambda e, i=i, xi=xi: e.dma_start(
                out=xi, in_=x_src.ap()[i * 128:(i + 1) * 128, :]),
                waits=[prev_xuse[sl]]
                + (FW if b == 0 and it < 2 else []),
                sig="dma", dkey=("xin", sl))
            tln = ln_tile(P, c, xi, xbs,
                          [ti] + (prev_epi if it < 2 else []), slot=sl)
            prev_xuse[sl] = tln
            for half in range(2):
                poff = 1024 if half == 0 else 0
                bkey = "B" if half == 0 else "A"
                tt = None
                for jj4 in range(4):
                    jj = half * 4 + jj4
                    tt = P.e("tensor", lambda e, jj=jj, jj4=jj4, poff=poff,
                             xbs=xbs: e.transpose(
                                 c.psumT[:, poff + jj4 * 128:
                                         poff + (jj4 + 1) * 128],
                                 xbs[:, jj * 128:(jj + 1) * 128],
                                 c.identB[:]),
                             waits=([tln, c.tok_consts, c.bank_tok.get(bkey)]
                                    + (prev_epi if it < 2 else []))
                             if jj4 == 0 else [],
                             sig="c" if jj4 == 3 else None)
                dsth = bass.AP(xhat.tensor,
                               xhat.offset + (half * 4) * 2048 + it * 128,
                               [[A16.n, 128], [2048, 4], [1, 128]])
                ev = P.e("scalar", lambda e, poff=poff, dsth=dsth: e.copy(
                    out=dsth, in_=c.psumT[:, poff:poff + 512]),
                    waits=[tt], sig="c")
                c.bank_tok[bkey] = ev
                ev_all.append(ev)
        xhat_ready = ev_all[-1]

        qkv_done = None
        for part, dst_d, obase in (("q", q_d, 0), ("k", k_d, 1024)):
            for mt in range(8):
                sb_ = mt % 2
                wb = wqb[:, sb_ * 1024:(sb_ + 1) * 1024]
                tw = P.e("gpsimd", lambda e, mt=mt, obase=obase, wb=wb:
                         e.dma_start(
                             out=wb, in_=bass.AP(
                                 ins["L0_attn_Wqkv"],
                                 obase + mt * 128,
                                 [[3072, 128], [128 * 3072, 8], [1, 128]])),
                         waits=[qkv_w_tok[sb_]]
                         + (FW if qkv_w_tok[sb_] is None else []),
                         sig="dma", dkey=("wq", sb_))
                for ncx in range(4):
                    pb = c.psum[2] if ncx % 2 == 0 else c.psum[5]
                    bk = "p2" if ncx % 2 == 0 else "p5"
                    mm = None
                    for kd in range(8):
                        mm = P.e("tensor", lambda e, kd=kd, ncx=ncx, pb=pb,
                                 wb=wb: e.matmul(
                                     pb[:, 0:512],
                                     wb[:, kd * 128:(kd + 1) * 128],
                                     xhat[:, kd * 2048 + ncx * 512:
                                          kd * 2048 + ncx * 512 + 512],
                                     start=(kd == 0), stop=(kd == 7)),
                                 waits=[tw, xhat_ready,
                                        c.bank_tok.get(bk)]
                                 if kd == 0 else [],
                                 sig="c" if kd == 7 else None)
                    qkv_w_tok[sb_] = mm
                    ssl = ncx % 2
                    bcol = mt + (0 if part == "q" else 8)
                    evq = P.e("vector", lambda e, pb=pb, ssl=ssl, bcol=bcol:
                              e.tensor_scalar(
                                  out=stq[:, ssl * 512:(ssl + 1) * 512],
                                  in0=pb[:, 0:512],
                                  scalar1=bqf[:, bcol:bcol + 1],
                                  scalar2=None, op0=ALU.add),
                              waits=[mm, tbl, stq_tok[ssl]], sig="c")
                    c.bank_tok[bk] = evq
                    std = P.e("sync", lambda e, mt=mt, ncx=ncx, ssl=ssl,
                              dst_d=dst_d: e.dma_start(
                                  out=bass.AP(dst_d,
                                              mt * 128 * T + ncx * 512,
                                              [[T, 128], [1, 512]]),
                                  in_=stq[:, ssl * 512:(ssl + 1) * 512]),
                              waits=[evq, c.bank_tok.get("qpl0"),
                                     c.bank_tok.get("qpl1")],
                              sig="dma", dkey=("qks", ssl))
                    stq_tok[ssl] = std
        for mt in range(8):
            sb_ = mt % 2
            wb = wqb[:, sb_ * 1024:(sb_ + 1) * 1024]
            tw = P.e("gpsimd", lambda e, mt=mt, wb=wb: e.dma_start(
                out=wb, in_=bass.AP(ins["L0_attn_Wqkv"], 2048 + mt * 128,
                                    [[3072, 128], [128 * 3072, 8],
                                     [1, 128]])),
                waits=[qkv_w_tok[sb_]], sig="dma", dkey=("wq", sb_))
            for tt8 in range(4):
                pb = c.psum[2] if tt8 % 2 == 0 else c.psum[5]
                bk = "p2" if tt8 % 2 == 0 else "p5"
                for st4 in range(4):
                    tti = tt8 * 4 + st4
                    mm = None
                    for kd in range(8):
                        mm = P.e("tensor", lambda e, kd=kd, tti=tti, pb=pb,
                                 st4=st4, wb=wb: e.matmul(
                                     pb[:, st4 * 128:(st4 + 1) * 128],
                                     xhat[:, kd * 2048 + tti * 128:
                                          kd * 2048 + tti * 128 + 128],
                                     wb[:, kd * 128:(kd + 1) * 128],
                                     start=(kd == 0), stop=(kd == 7)),
                                 waits=[tw, xhat_ready,
                                        c.bank_tok.get(bk)]
                                 if kd == 0 and st4 == 0 else [],
                                 sig="c" if kd == 7 and st4 == 3 else None)
                qkv_w_tok[sb_] = mm
                ssl = tt8 % 2
                evv = None
                for st4 in range(4):
                    evv = P.e("vector", lambda e, pb=pb, ssl=ssl, mt=mt,
                              st4=st4: e.tensor_tensor(
                                  out=stq[:, ssl * 512 + st4 * 128:
                                          ssl * 512 + (st4 + 1) * 128],
                                  in0=pb[:, st4 * 128:(st4 + 1) * 128],
                                  in1=bvb[:, mt * 128:(mt + 1) * 128],
                                  op=ALU.add),
                              waits=[mm, tbl, stq_tok[ssl]], sig="c")
                c.bank_tok[bk] = evv
                std = P.e("sync", lambda e, mt=mt, tt8=tt8, ssl=ssl:
                          e.dma_start(
                              out=bass.AP(
                                  v_d, tt8 * 512 * 1024 + mt * 128,
                                  [[1024, 128], [128 * 1024, 4], [1, 128]]),
                              in_=stq[:, ssl * 512:(ssl + 1) * 512]),
                          waits=[evv], sig="dma", dkey=("qks", ssl))
                stq_tok[ssl] = std
        qkv_done_toks = [(P.dma_sem(("qks", i)),
                          P.cnt[P.dma_sem(("qks", i))], "#") for i in (0, 1)]

        for hp8 in range(8):
            hp = hp8 % 2
            h0 = 2 * hp8
            tq = P.e("scalar", lambda e, hp8=hp8, hp=hp: e.dma_start(
                out=qpb[:, hp * 2048:(hp + 1) * 2048],
                in_=bass.AP(q_d, hp8 * 128 * T, [[T, 128], [1, T]])),
                waits=[pair_tok[hp]] + qkv_done_toks, sig="dma",
                dkey=("qpl", hp))
            tk = P.e("scalar", lambda e, hp8=hp8, hp=hp: e.dma_start(
                out=kpb[:, hp * 2048:(hp + 1) * 2048],
                in_=bass.AP(k_d, hp8 * 128 * T, [[T, 128], [1, T]])),
                waits=[pair_tok[hp]] + qkv_done_toks, sig="dma",
                dkey=("qpl", hp))
            tv = P.e("scalar", lambda e, hp8=hp8, hp=hp: e.dma_start(
                out=vpb[:, hp * 2048:(hp + 1) * 2048],
                in_=bass.AP(v_d, hp8 * 128,
                            [[1024, 128], [128 * 1024, 16], [1, 128]])),
                waits=[pair_tok[hp]] + qkv_done_toks, sig="dma",
                dkey=("qpl", hp))
            tpl = tv
            c.bank_tok["qpl%d" % hp] = tv
            for sub in range(2):
                h = h0 + sub
                base = sub * 64
                prev_texp = None
                mmo = None
                for qt in range(16):
                    njb = qt // 4 + 1
                    sev = []
                    btok = None
                    for jb in range(njb):
                        pb = c.psum[2] if jb % 2 == 0 else c.psum[5]
                        bk = "p2" if jb % 2 == 0 else "p5"
                        mm = P.e("tensor", lambda e, qt=qt, jb=jb, pb=pb,
                                 base=base, hp=hp: e.matmul(
                                     pb[:, 0:512],
                                     bass.AP(qpb.tensor,
                                             qpb.offset + hp * 2048
                                             + qt * 128 + base * A16.n,
                                             [[A16.n, 64], [1, 128]]),
                                     bass.AP(kpb.tensor,
                                             kpb.offset + hp * 2048
                                             + jb * 512 + base * A16.n,
                                             [[A16.n, 64], [1, 512]]),
                                     start=True, stop=True),
                                 waits=[tpl, c.bank_tok.get(bk)], sig="c")
                        tbt = P.e("gpsimd", lambda e, h=h, qt=qt, jb=jb:
                                  e.dma_start(
                                      out=btile[:, 0:512],
                                      in_=bass.AP(bias_d,
                                                  h * T * T + qt * 128 * T
                                                  + jb * 512,
                                                  [[T, 128], [1, 512]])),
                                  waits=[btok, c.bank_tok.get("bt")]
                                  + (bias_done_toks if qt == 0 and jb == 0
                                     else []),
                                  sig="dma", dkey=("btl", hp))
                        btok = tbt
                        _pt = prev_texp
                        if jb == qt // 4:
                            ev = P.e("vector", lambda e, jb=jb, pb=pb:
                                     e.tensor_tensor(
                                         out=Sf[:, jb * 512:(jb + 1) * 512],
                                         in0=pb[:, 0:512],
                                         in1=btile[:, 0:512],
                                         op=ALU.add),
                                     waits=[mm, tbt, _pt], sig="c")
                            off = 384 - (qt * 128 - jb * 512)
                            ev = P.e("vector", lambda e, jb=jb, off=off:
                                     e.tensor_tensor(
                                         out=Sf[:, jb * 512:(jb + 1) * 512],
                                         in0=Sf[:, jb * 512:(jb + 1) * 512],
                                         in1=c.diagmask[:, off:off + 512],
                                         op=ALU.add),
                                     waits=[ev, c.tok_consts], sig="c")
                        else:
                            ev = P.e("vector", lambda e, jb=jb, pb=pb:
                                     e.tensor_tensor(
                                         out=Sf[:, jb * 512:(jb + 1) * 512],
                                         in0=pb[:, 0:512],
                                         in1=btile[:, 0:512],
                                         op=ALU.add),
                                     waits=[mm, tbt, _pt], sig="c")
                        c.bank_tok["bt"] = ev
                        c.bank_tok[bk] = ev
                        sev.append(ev)
                    kvlen = njb * 512
                    tnm = P.e("vector", lambda e, kvlen=kvlen: e.reduce_max(
                        out=rl[:, 0:1], in_=Sf[:, 0:kvlen], axis=AX,
                        negate=True), waits=sev, sig="c")
                    tnm8 = P.e("vector", lambda e: e.tensor_scalar(
                        out=rl[:, 1:2], in0=rl[:, 0:1], scalar1=0.125,
                        scalar2=None, op0=ALU.mult), waits=[tnm], sig="c")
                    texp = P.e("scalar", lambda e, kvlen=kvlen: e.activation(
                        out=Pb[:, 0:kvlen], in_=Sf[:, 0:kvlen], func=AF.Exp,
                        scale=0.125, bias=rl[:, 1:2], accum_out=rl[:, 2:3]),
                        waits=[tnm8] + sev, sig="c")
                    prev_texp = texp
                    trl = P.e("vector", lambda e: e.reciprocal(
                        out=rl[:, 3:4], in_=rl[:, 2:3]),
                        waits=[texp], sig="c")
                    tns = P.e("gpsimd", lambda e, kvlen=kvlen:
                              e.tensor_scalar(
                                  out=Pb[:, 0:kvlen], in0=Pb[:, 0:kvlen],
                                  scalar1=rl[:, 3:4], scalar2=None,
                                  op0=ALU.mult), waits=[trl], sig="c")
                    nkt = njb * 4 - (3 - (qt % 4))
                    for g4 in range((nkt + 3) // 4):
                        poff = 1024 if g4 % 2 == 0 else 0
                        bkey = "B" if g4 % 2 == 0 else "A"
                        tt = None
                        jmax = min(4, nkt - g4 * 4)
                        for j in range(jmax):
                            kt = g4 * 4 + j
                            tt = P.e("tensor", lambda e, kt=kt, j=j,
                                     poff=poff: e.transpose(
                                         c.psumT[:, poff + j * 128:
                                                 poff + (j + 1) * 128],
                                         Pb[:, kt * 128:(kt + 1) * 128],
                                         c.identB[:]),
                                     waits=[tns, c.tok_consts,
                                            c.bank_tok.get(bkey)]
                                     if j == 0 else [],
                                     sig="c" if j == jmax - 1 else None)
                        evp = P.e("scalar", lambda e, g4=g4, poff=poff,
                                  jmax=jmax: e.copy(
                                      out=PT[:, g4 * 512:
                                             g4 * 512 + jmax * 128],
                                      in_=c.psumT[:, poff:
                                                  poff + jmax * 128]),
                                  waits=[tt], sig="c")
                        c.bank_tok[bkey] = evp
                        for j in range(jmax):
                            kt = g4 * 4 + j
                            mmo = P.e("tensor", lambda e, kt=kt, j=j,
                                      sub=sub, hp=hp, nkt=nkt: e.matmul(
                                          c.psum[4][0:64, 0:128],
                                          vpb[:, hp * 2048 + kt * 128
                                              + sub * 64:
                                              hp * 2048 + kt * 128
                                              + sub * 64 + 64],
                                          PT[:, kt * 128:(kt + 1) * 128],
                                          start=(kt == 0),
                                          stop=(kt == nkt - 1)),
                                      waits=[evp, tpl,
                                             c.bank_tok.get("p4")]
                                      if j == 0 else [],
                                      sig="c" if kt == nkt - 1 else None)
                    evo = P.e("vector", lambda e, qt=qt, h=h: e.tensor_copy(
                        out=bass.AP(yfm.tensor,
                                    yfm.offset + (h // 2) * 2048 + qt * 128
                                    + (h % 2) * 64 * A16.n,
                                    [[A16.n, 64], [1, 128]]),
                        in_=c.psum[4][0:64, 0:128]),
                        waits=[mmo] + (prev_epi if qt == 0 else []),
                        sig="c")
                    c.bank_tok["p4"] = evo
            pair_tok[hp] = mmo
        yfm_ready = c.bank_tok["p4"]

        prev_t3 = None
        for it in range(16):
            i = b * 16 + it
            sl = it % 2
            xi = xep[:, sl * 1024:(sl + 1) * 1024]
            txi = P.e("sync", lambda e, i=i, xi=xi: e.dma_start(
                out=xi, in_=x_src.ap()[i * 128:(i + 1) * 128, :]),
                waits=[prev_store[sl]], sig="dma", dkey=("xep", sl))
            ptoks = []
            for ocx in range(2):
                pb = c.psum[2] if ocx == 0 else c.psum[5]
                bk = "p2" if ocx == 0 else "p5"
                wb = wpb[:, ocx * 4096:(ocx + 1) * 4096]
                if it == 0 and b == 0:
                    tw = P.e("gpsimd", lambda e, ocx=ocx, wb=wb: e.dma_start(
                        out=wb, in_=bass.AP(ins["L0_attn_Wproj"],
                                            ocx * 512,
                                            [[1024, 128], [128 * 1024, 8],
                                             [1, 512]])),
                        waits=[qkv_w_tok[ocx]], sig="dma", dkey=("wp", ocx))
                    c.bank_tok["wp%d" % ocx] = tw
                tw_t = c.bank_tok.get("wp%d" % ocx)
                mm = None
                for kd in range(8):
                    mm = P.e("tensor", lambda e, kd=kd, ocx=ocx, it=it,
                             pb=pb, wb=wb: e.matmul(
                                 pb[:, 0:512],
                                 yfm[:, kd * 2048 + it * 128:
                                     kd * 2048 + it * 128 + 128],
                                 wb[:, kd * 512:(kd + 1) * 512],
                                 start=(kd == 0), stop=(kd == 7)),
                             waits=[tw_t, yfm_ready, c.bank_tok.get(bk)]
                             if kd == 0 else [],
                             sig="c" if kd == 7 else None)
                ev = P.e("vector", lambda e, ocx=ocx, pb=pb: e.tensor_tensor(
                    out=yts[:, ocx * 512:(ocx + 1) * 512], in0=pb[:, 0:512],
                    in1=bpb[:, ocx * 512:(ocx + 1) * 512], op=ALU.add),
                    waits=[mm, tbl, prev_store[sl], prev_t3], sig="c")
                c.bank_tok[bk] = ev
                ptoks.append(ev)
            t3 = ln_tile(P, c, yts, yns,
                         ptoks + [prev_store[0], prev_store[1]], slot=sl)
            prev_t3 = t3
            t4 = P.e("vector", lambda e, xi=xi: e.tensor_tensor(
                out=yns, in0=yns, in1=xi, op=ALU.add),
                waits=[t3, txi], sig="c")
            stt = P.e("sync", lambda e, i=i: e.dma_start(
                out=x_dst.ap()[i * 128:(i + 1) * 128, :], in_=yns),
                waits=[t4], sig="dma", dkey=("xout", sl))
            prev_store[sl] = stt
        prev_epi = [P.prev_c["vector"]]
    A32.pop()
    A16.pop()
    P.barrier()
    c.arena_war = None





def emit_l0_ffn(P, c, x_src, x_dst):
    nc = P.nc
    ins = c.ins
    A32, A16 = c.A32, c.A16
    aw = c.arena_war
    h0_d = _dram(nc, "h0_d", (B * T, D), F32)
    x2_d = _dram(nc, "x2_d", (B * T, D), F32)
    acc1_d = _dram(nc, "acc1_d", (B * T, D), F32)
    q1_d = _dram(nc, "q1_d", (B, 1024, T), BF16)
    k1_d = _dram(nc, "k1_d", (B, 512, T), BF16)
    v1_d = _dram(nc, "v1_d", (B, T, 512), BF16)
    wg_b = _dram(nc, "wg_b", (8, 1024, 3584), BF16)
    wu_b = _dram(nc, "wu_b", (8, 1024, 3584), BF16)
    wd_b = _dram(nc, "wd_b", (8, 3584, 1024), BF16)

    A32.push()
    A16.push()
    wc = None
    for e_i in range(8):
        for srcw, dstw in ((ins["L0_ffn_w_gate"], wg_b),
                           (ins["L0_ffn_w_up"], wu_b)):
            wc = P.e("gpsimd", lambda e, e_i=e_i, srcw=srcw, dstw=dstw:
                     e.dma_start(
                         out=bass.AP(dstw, e_i * 1024 * 3584,
                                     [[3584, 1024], [1, 3584]]),
                         in_=bass.AP(srcw, e_i * 1024 * 3584,
                                     [[3584, 1024], [1, 3584]])),
                     waits=[aw] if e_i == 0 else [], sig="dma",
                     dkey=("wcst",))
        wc = P.e("gpsimd", lambda e, e_i=e_i: e.dma_start(
            out=bass.AP(wd_b, e_i * 3584 * 1024,
                        [[1024, 3584], [1, 1024]]),
            in_=bass.AP(ins["L0_ffn_w_down"], e_i * 3584 * 1024,
                        [[1024, 3584], [1, 1024]])),
            waits=[], sig="dma", dkey=("wcst",))
    wcast_done = [(P.dma_sem(("wcst",)), P.cnt[P.dma_sem(("wcst",))], "#")]

    ctab = A32.take(2048)
    stab = A32.take(2048)
    rcol = A32.take(8)
    lnw1f = A32.take(8)
    lnw2f = A32.take(8)
    comb = A32.take(128)
    lgt = A32.take(32)
    rl = A32.take(8)
    rwb = A16.take(64)
    h2hat = A16.take(16384)

    tl1 = P.e("sync", lambda e: e.dma_start(
        out=lnw1f, in_=bass.AP(ins["L0_ffn_ln1_w"], 0,
                               [[1, 128], [128, 8], [1, 1]]),
        allow_slow_non_contiguous=True), waits=[aw], sig="dma", dkey=("lnw",))
    tl2 = P.e("sync", lambda e: e.dma_start(
        out=lnw2f, in_=bass.AP(ins["L0_ffn_ln2_w"], 0,
                               [[1, 128], [128, 8], [1, 1]]),
        allow_slow_non_contiguous=True), waits=[aw], sig="dma", dkey=("lnw",))
    trw = P.e("gpsimd", lambda e: e.dma_start(
        out=rwb, in_=bass.AP(ins["L0_ffn_router_w"], 0,
                             [[8, 128], [1024, 8], [1, 8]])),
        waits=[aw], sig="dma", dkey=("rwl",))

    itmp = nc.alloc_sbuf_tensor("rope_itmp", [128, 512], I32)
    tp = P.e("gpsimd", lambda e: e.iota(
        rcol[:, 0:1], pattern=[[1, 1]], base=0, channel_multiplier=1,
        allow_small_or_imprecise_dtypes=True), waits=[aw], sig="c")
    tca = P.e("vector", lambda e: e.tensor_scalar(
        out=rcol[:, 3:4], in0=rcol[:, 0:1], scalar1=32.0, scalar2=None,
        op0=ALU.is_ge), waits=[tp], sig="c")
    tcb = P.e("vector", lambda e: e.tensor_scalar(
        out=rcol[:, 4:5], in0=rcol[:, 0:1], scalar1=64.0, scalar2=None,
        op0=ALU.is_ge), waits=[tp], sig="c")
    tcc = P.e("vector", lambda e: e.tensor_scalar(
        out=rcol[:, 5:6], in0=rcol[:, 0:1], scalar1=96.0, scalar2=None,
        op0=ALU.is_ge), waits=[tp], sig="c")
    tsg = P.e("vector", lambda e: e.scalar_tensor_tensor(
        out=rcol[:, 2:3], in0=rcol[:, 4:5], scalar=-64.0,
        in1=rcol[:, 0:1], op0=ALU.mult, op1=ALU.add),
        waits=[tcb], sig="c")
    tsg = P.e("vector", lambda e: e.tensor_scalar(
        out=rcol[:, 2:3], in0=rcol[:, 2:3], scalar1=32.0, scalar2=None,
        op0=ALU.is_lt), waits=[tsg], sig="c")
    tsg = P.e("vector", lambda e: e.tensor_scalar(
        out=rcol[:, 2:3], in0=rcol[:, 2:3], scalar1=2.0, scalar2=-1.0,
        op0=ALU.mult, op1=ALU.add), waits=[tsg], sig="c")
    tif = P.e("vector", lambda e: e.tensor_tensor(
        out=rcol[:, 3:4], in0=rcol[:, 3:4], in1=rcol[:, 4:5], op=ALU.add),
        waits=[tca, tcb], sig="c")
    tif = P.e("vector", lambda e: e.tensor_tensor(
        out=rcol[:, 3:4], in0=rcol[:, 3:4], in1=rcol[:, 5:6], op=ALU.add),
        waits=[tif, tcc], sig="c")
    tif = P.e("vector", lambda e: e.scalar_tensor_tensor(
        out=rcol[:, 1:2], in0=rcol[:, 3:4], scalar=-32.0,
        in1=rcol[:, 0:1], op0=ALU.mult, op1=ALU.add), waits=[tif], sig="c")
    tif = P.e("scalar", lambda e: e.activation(
        out=rcol[:, 1:2], in_=rcol[:, 1:2], func=AF.Exp,
        scale=-math.log(10000.0) / 32.0), waits=[tif], sig="c")
    PI = math.pi
    ttab = None
    for ncx in range(4):
        t0_ = P.e("gpsimd", lambda e, ncx=ncx: e.iota(
            c.tmp2k[:, 0:512], pattern=[[1, 512]], base=ncx * 512,
            channel_multiplier=0, allow_small_or_imprecise_dtypes=True),
            waits=[c.tok_consts if ncx == 0 else ttab], sig="c")
        ta_ = P.e("vector", lambda e: e.tensor_scalar(
            out=c.tmp2k[:, 512:1024], in0=c.tmp2k[:, 0:512],
            scalar1=rcol[:, 1:2], scalar2=None, op0=ALU.mult),
            waits=[t0_, tif], sig="c")
        tsn = None
        mPI, tmPI = constap(P, c, -math.pi)
        for path in range(2):
            dst = stab if path == 0 else ctab
            tu = P.e("vector", lambda e, path=path: e.tensor_scalar(
                out=c.tmp2k[:, 1024:1536], in0=c.tmp2k[:, 512:1024],
                scalar1=1.0 / (2 * PI),
                scalar2=0.25 * path, op0=ALU.mult, op1=ALU.add),
                waits=[ta_, tsn], sig="c")
            tic = P.e("vector", lambda e: e.tensor_copy(
                out=itmp[:], in_=c.tmp2k[:, 1024:1536]), waits=[tu],
                sig="c")
            tfc = P.e("vector", lambda e: e.tensor_copy(
                out=c.tmp2k[:, 1536:2048], in_=itmp[:]), waits=[tic],
                sig="c")
            tgt = P.e("vector", lambda e: e.tensor_tensor(
                out=c.tmp2k[:, 0:512], in0=c.tmp2k[:, 1536:2048],
                in1=c.tmp2k[:, 1024:1536], op=ALU.is_gt),
                waits=[tfc], sig="c")
            tfl = P.e("vector", lambda e: e.tensor_tensor(
                out=c.tmp2k[:, 1536:2048], in0=c.tmp2k[:, 1536:2048],
                in1=c.tmp2k[:, 0:512], op=ALU.subtract),
                waits=[tgt], sig="c")
            tm = P.e("vector", lambda e: e.tensor_tensor(
                out=c.tmp2k[:, 1024:1536], in0=c.tmp2k[:, 1024:1536],
                in1=c.tmp2k[:, 1536:2048], op=ALU.subtract),
                waits=[tfl], sig="c")
            tsn = P.e("scalar", lambda e, ncx=ncx, dst=dst: e.activation(
                out=dst[:, ncx * 512:(ncx + 1) * 512],
                in_=c.tmp2k[:, 1024:1536], func=AF.Sin, scale=2 * PI,
                bias=mPI), waits=[tm, tmPI], sig="c")
        tt1 = P.e("vector", lambda e, ncx=ncx: e.tensor_scalar(
            out=stab[:, ncx * 512:(ncx + 1) * 512],
            in0=stab[:, ncx * 512:(ncx + 1) * 512],
            scalar1=rcol[:, 2:3], scalar2=None, op0=ALU.mult),
            waits=[tsn, tsg], sig="c")
        ttab = P.e("vector", lambda e, ncx=ncx: e.tensor_scalar(
            out=ctab[:, ncx * 512:(ncx + 1) * 512],
            in0=ctab[:, ncx * 512:(ncx + 1) * 512],
            scalar1=-1.0, scalar2=None, op0=ALU.mult),
            waits=[tt1], sig="c")

    prev_xuse = [None, None]
    prev_h0use = [None, None]
    prev_tt = [None, None]
    prev_x2use = [None, None]
    prev_tt2 = [None, None]
    prev_h0t = [None, None]
    prevF = [None, None]
    prevF2 = [None, None]
    prevF3 = [None, None]
    wpb_tok = [None, None]
    wvb_tok = None
    pair_tok = [None, None]
    acc_tok = {}
    comb_war = None
    for b in range(B):
        entry = [aw] if b == 0 else [prevF[0], prevF[1], prevF2[0],
                                     prevF2[1], prevF3[0], prevF3[1]]
        A16.push()
        yfm = A16.take(16384)
        A16.push()
        h1hat = A16.take(16384)
        xhb = A16.take(2048)
        wqb = A16.take(2048)
        wqsb = A16.take(2048)
        wvb = A16.take(4096)
        stq = A16.take(1024)

        A32.push()
        xin = A32.take(2048)
        h0b = A32.take(2048)
        ev_all = []
        for it in range(16):
            i = b * 16 + it
            sl = it % 2
            xi = xin[:, sl * 1024:(sl + 1) * 1024]
            h0i = h0b[:, sl * 1024:(sl + 1) * 1024]
            xbs = xhb[:, sl * 1024:(sl + 1) * 1024]
            ti = P.e("sync", lambda e, i=i, xi=xi: e.dma_start(
                out=xi, in_=x_src.ap()[i * 128:(i + 1) * 128, :]),
                waits=[prev_xuse[sl]] + (entry if it < 2 else []),
                sig="dma", dkey=("xin", sl))
            tln = ln_tile(P, c, xi, h0i,
                          [ti] + (prev_h0use[sl] or [])
                          + (entry if it < 2 else []), slot=sl)
            prev_xuse[sl] = tln
            th0 = P.e("sync", lambda e, i=i, h0i=h0i: e.dma_start(
                out=h0_d.ap()[i * 128:(i + 1) * 128, :], in_=h0i),
                waits=[tln], sig="dma", dkey=("h0w", sl))
            trm = ln_tile(P, c, h0i, xbs,
                          [tln, prev_tt[sl]], slot=sl, rms=True, eps=1e-6)
            prev_h0use[sl] = [trm, th0]
            for half in range(2):
                poff = 1024 if half == 0 else 0
                bkey = "B" if half == 0 else "A"
                tt = None
                for j in range(4):
                    jj = half * 4 + j
                    tt = P.e("tensor", lambda e, jj=jj, j=j, poff=poff,
                             xbs=xbs: e.transpose(
                                 c.psumT[:, poff + j * 128:
                                         poff + (j + 1) * 128],
                                 xbs[:, jj * 128:(jj + 1) * 128],
                                 c.identB[:]),
                             waits=[trm, c.tok_consts, c.bank_tok.get(bkey)]
                             if j == 0 else [],
                             sig="c" if j == 3 else None)
                dsth = bass.AP(h1hat.tensor,
                               h1hat.offset + (half * 4) * 2048 + it * 128,
                               [[A16.n, 128], [2048, 4], [1, 128]])
                ev = P.e("scalar", lambda e, poff=poff, dsth=dsth: e.copy(
                    out=dsth, in_=c.psumT[:, poff:poff + 512]),
                    waits=[tt], sig="c")
                c.bank_tok[bkey] = ev
                ev_all.append(ev)
                prev_tt[sl] = tt
        tsc1 = None
        for kd in range(8):
            tsc1 = P.e("vector", lambda e, kd=kd: e.tensor_scalar(
                out=h1hat[:, kd * 2048:(kd + 1) * 2048],
                in0=h1hat[:, kd * 2048:(kd + 1) * 2048],
                scalar1=lnw1f[:, kd:kd + 1], scalar2=None, op0=ALU.mult),
                waits=[ev_all[-1], tl2] if kd == 0 else [], sig="c")
        A32.pop()

        wq_tok = [None, None]
        ws_tok = [None, None]
        stq_tok = [None, None]
        for part, nmt, wname, rowlen, dst_d in (
                ("q", 8, "L0_ffn_wq", 1024, q1_d),
                ("k", 4, "L0_ffn_wk", 512, k1_d)):
            for mt in range(nmt):
                sb_ = mt % 2
                wb = wqb[:, sb_ * 1024:(sb_ + 1) * 1024]
                wsb = wqsb[:, sb_ * 1024:(sb_ + 1) * 1024]
                twn = P.e("gpsimd", lambda e, mt=mt, wb=wb, wname=wname,
                          rowlen=rowlen: e.dma_start(
                              out=wb,
                              in_=bass.AP(ins[wname], mt * 128,
                                          [[rowlen, 128],
                                           [128 * rowlen, 8], [1, 128]])),
                          waits=[wq_tok[sb_]]
                          + ([aw] if wq_tok[sb_] is None and b == 0 else []),
                          sig="dma", dkey=("wq", sb_))
                tws = None
                for di, (dof, sof) in enumerate(
                        ((0, 32), (32, 0), (64, 96), (96, 64))):
                    tws = P.e("gpsimd", lambda e, mt=mt, dof=dof, sof=sof,
                              wsb=wsb, wname=wname, rowlen=rowlen:
                              e.dma_start(
                                  out=bass.AP(wsb.tensor, wsb.offset + dof,
                                              [[A16.n, 128], [128, 8],
                                               [1, 32]]),
                                  in_=bass.AP(ins[wname], mt * 128 + sof,
                                              [[rowlen, 128],
                                               [128 * rowlen, 8], [1, 32]])),
                              waits=[ws_tok[sb_]] if di == 0 else [],
                              sig="dma", dkey=("ws", sb_))
                mmn = mms = None
                for ncx in range(4):
                    ssl = ncx % 2
                    for kd in range(8):
                        mmn = P.e("tensor", lambda e, kd=kd, ncx=ncx, wb=wb:
                                  e.matmul(
                                      c.psum[2][:, 0:512],
                                      wb[:, kd * 128:(kd + 1) * 128],
                                      h1hat[:, kd * 2048 + ncx * 512:
                                            kd * 2048 + ncx * 512 + 512],
                                      start=(kd == 0), stop=(kd == 7)),
                                  waits=[twn, tsc1, c.bank_tok.get("p2")]
                                  if kd == 0 else [],
                                  sig="c" if kd == 7 else None)
                    for kd in range(8):
                        mms = P.e("tensor", lambda e, kd=kd, ncx=ncx,
                                  wsb=wsb: e.matmul(
                                      c.psum[5][:, 0:512],
                                      wsb[:, kd * 128:(kd + 1) * 128],
                                      h1hat[:, kd * 2048 + ncx * 512:
                                            kd * 2048 + ncx * 512 + 512],
                                      start=(kd == 0), stop=(kd == 7)),
                                  waits=[tws, tsc1, c.bank_tok.get("p5")]
                                  if kd == 0 else [],
                                  sig="c" if kd == 7 else None)
                    ev1 = P.e("vector", lambda e, ncx=ncx: e.tensor_tensor(
                        out=c.tmp2k[:, 0:512], in0=c.psum[2][:, 0:512],
                        in1=ctab[:, ncx * 512:(ncx + 1) * 512], op=ALU.mult),
                        waits=[mmn, ttab], sig="c")
                    c.bank_tok["p2"] = ev1
                    ev2 = P.e("vector", lambda e, ncx=ncx: e.tensor_tensor(
                        out=c.tmp2k[:, 512:1024], in0=c.psum[5][:, 0:512],
                        in1=stab[:, ncx * 512:(ncx + 1) * 512], op=ALU.mult),
                        waits=[mms], sig="c")
                    c.bank_tok["p5"] = ev2
                    ev3 = P.e("vector", lambda e, ssl=ssl: e.tensor_tensor(
                        out=stq[:, ssl * 512:(ssl + 1) * 512],
                        in0=c.tmp2k[:, 0:512], in1=c.tmp2k[:, 512:1024],
                        op=ALU.add), waits=[ev2, stq_tok[ssl]], sig="c")
                    std = P.e("sync", lambda e, mt=mt, ncx=ncx, ssl=ssl,
                              dst_d=dst_d, b=b: e.dma_start(
                                  out=bass.AP(dst_d,
                                              b * (1024 if dst_d is q1_d
                                                   else 512) * T
                                              + mt * 128 * T + ncx * 512,
                                              [[T, 128], [1, 512]]),
                                  in_=stq[:, ssl * 512:(ssl + 1) * 512]),
                              waits=[ev3, c.bank_tok.get("qpl0"),
                                     c.bank_tok.get("qpl1")],
                              sig="dma", dkey=("qks", ssl))
                    stq_tok[ssl] = std
                wq_tok[sb_] = mmn
                ws_tok[sb_] = mms
        twv = P.e("gpsimd", lambda e: e.dma_start(
            out=wvb, in_=bass.AP(ins["L0_ffn_wv"], 0,
                                 [[512, 128], [128 * 512, 8], [1, 512]])),
            waits=[wvb_tok] + ([aw] if b == 0 else []), sig="dma",
            dkey=("wv",))
        mm = None
        for tt_ in range(16):
            pb = c.psum[2] if tt_ % 2 == 0 else c.psum[5]
            bk = "p2" if tt_ % 2 == 0 else "p5"
            ssl = tt_ % 2
            for kd in range(8):
                mm = P.e("tensor", lambda e, kd=kd, tt_=tt_, pb=pb:
                         e.matmul(
                             pb[:, 0:512],
                             h1hat[:, kd * 2048 + tt_ * 128:
                                   kd * 2048 + tt_ * 128 + 128],
                             wvb[:, kd * 512:(kd + 1) * 512],
                             start=(kd == 0), stop=(kd == 7)),
                         waits=[twv, tsc1, c.bank_tok.get(bk)]
                         if kd == 0 else [],
                         sig="c" if kd == 7 else None)
            ev = P.e("scalar", lambda e, pb=pb, ssl=ssl: e.copy(
                out=stq[:, ssl * 512:(ssl + 1) * 512], in_=pb[:, 0:512]),
                waits=[mm, stq_tok[ssl]], sig="c")
            c.bank_tok[bk] = ev
            std = P.e("sync", lambda e, tt_=tt_, ssl=ssl, b=b: e.dma_start(
                out=bass.AP(v1_d, b * T * 512 + tt_ * 128 * 512,
                            [[512, 128], [1, 512]]),
                in_=stq[:, ssl * 512:(ssl + 1) * 512]),
                waits=[ev, c.bank_tok.get("qpl0"), c.bank_tok.get("qpl1")],
                sig="dma", dkey=("qks", ssl))
            stq_tok[ssl] = std
        wvb_tok = mm
        qkv_done_toks = [(P.dma_sem(("qks", i)),
                          P.cnt[P.dma_sem(("qks", i))], "#") for i in (0, 1)]
        A16.pop()

        A32.push()
        Sf = A32.take(2048)
        A16.push()
        qpb = A16.take(2 * 2048)
        kpb = A16.take(2 * 2048)
        vpb = A16.take(2 * 1024)
        Pb = A16.take(2048)
        PT = A16.take(2048)
        texp = None
        mmo = None
        for hp8 in range(8):
            hp = hp8 % 2
            tq = P.e("scalar", lambda e, hp8=hp8, hp=hp, b=b: e.dma_start(
                out=qpb[:, hp * 2048:(hp + 1) * 2048],
                in_=bass.AP(q1_d, b * 1024 * T + hp8 * 128 * T,
                            [[T, 128], [1, T]])),
                waits=[pair_tok[hp]] + qkv_done_toks, sig="dma",
                dkey=("qpl", hp))
            for h2 in range(2):
                tq = P.e("scalar", lambda e, hp8=hp8, hp=hp, h2=h2, b=b:
                         e.dma_start(
                             out=bass.AP(kpb.tensor,
                                         kpb.offset + hp * 2048
                                         + h2 * 64 * A16.n,
                                         [[A16.n, 64], [1, 2048]]),
                             in_=bass.AP(k1_d, b * 512 * T + hp8 * 64 * T,
                                         [[T, 64], [1, T]])),
                         waits=[], sig="dma", dkey=("qpl", hp))
            tv = P.e("scalar", lambda e, hp8=hp8, hp=hp, b=b: e.dma_start(
                out=vpb[:, hp * 1024:(hp + 1) * 1024],
                in_=bass.AP(v1_d, b * T * 512 + hp8 * 64,
                            [[512, 128], [128 * 512, 16], [1, 64]])),
                waits=[], sig="dma", dkey=("qpl", hp))
            tpl = tv
            c.bank_tok["qpl%d" % hp] = tv
            for sub in range(2):
                h = 2 * hp8 + sub
                prev_texp = texp
                for qt in range(16):
                    njb = qt // 4 + 1
                    sev = []
                    for jb in range(njb):
                        pb = c.psum[2] if jb % 2 == 0 else c.psum[5]
                        bk = "p2" if jb % 2 == 0 else "p5"
                        mm = P.e("tensor", lambda e, qt=qt, jb=jb, pb=pb,
                                 sub=sub, hp=hp: e.matmul(
                                     pb[:, 0:512],
                                     bass.AP(qpb.tensor,
                                             qpb.offset + hp * 2048
                                             + qt * 128 + sub * 64 * A16.n,
                                             [[A16.n, 64], [1, 128]]),
                                     bass.AP(kpb.tensor,
                                             kpb.offset + hp * 2048
                                             + jb * 512 + sub * 64 * A16.n,
                                             [[A16.n, 64], [1, 512]]),
                                     start=True, stop=True),
                                 waits=[tpl, c.bank_tok.get(bk)], sig="c")
                        _pt = prev_texp
                        if jb == qt // 4:
                            off = 384 - (qt * 128 - jb * 512)
                            ev = P.e("vector", lambda e, jb=jb, pb=pb,
                                     off=off: e.tensor_tensor(
                                         out=Sf[:, jb * 512:(jb + 1) * 512],
                                         in0=pb[:, 0:512],
                                         in1=c.diagmask[:, off:off + 512],
                                         op=ALU.add),
                                     waits=[mm, c.tok_consts, _pt], sig="c")
                        else:
                            ev = P.e("vector", lambda e, jb=jb, pb=pb:
                                     e.tensor_copy(
                                         out=Sf[:, jb * 512:(jb + 1) * 512],
                                         in_=pb[:, 0:512]),
                                     waits=[mm, _pt], sig="c")
                        c.bank_tok[bk] = ev
                        sev.append(ev)
                    kvlen = njb * 512
                    tnm = P.e("vector", lambda e, kvlen=kvlen: e.reduce_max(
                        out=rl[:, 0:1], in_=Sf[:, 0:kvlen], axis=AX,
                        negate=True), waits=sev, sig="c")
                    tnm8 = P.e("vector", lambda e: e.tensor_scalar(
                        out=rl[:, 1:2], in0=rl[:, 0:1], scalar1=0.125,
                        scalar2=None, op0=ALU.mult), waits=[tnm], sig="c")
                    texp = P.e("scalar", lambda e, kvlen=kvlen: e.activation(
                        out=Pb[:, 0:kvlen], in_=Sf[:, 0:kvlen], func=AF.Exp,
                        scale=0.125, bias=rl[:, 1:2], accum_out=rl[:, 2:3]),
                        waits=[tnm8] + sev, sig="c")
                    prev_texp = texp
                    trl = P.e("vector", lambda e: e.reciprocal(
                        out=rl[:, 3:4], in_=rl[:, 2:3]), waits=[texp],
                        sig="c")
                    tns = P.e("gpsimd", lambda e, kvlen=kvlen:
                              e.tensor_scalar(
                                  out=Pb[:, 0:kvlen], in0=Pb[:, 0:kvlen],
                                  scalar1=rl[:, 3:4], scalar2=None,
                                  op0=ALU.mult), waits=[trl], sig="c")
                    nkt = njb * 4 - (3 - (qt % 4))
                    for g4 in range((nkt + 3) // 4):
                        poff = 1024 if g4 % 2 == 0 else 0
                        bkey = "B" if g4 % 2 == 0 else "A"
                        jmax = min(4, nkt - g4 * 4)
                        tt = None
                        for j in range(jmax):
                            kt = g4 * 4 + j
                            tt = P.e("tensor", lambda e, kt=kt, j=j,
                                     poff=poff: e.transpose(
                                         c.psumT[:, poff + j * 128:
                                                 poff + (j + 1) * 128],
                                         Pb[:, kt * 128:(kt + 1) * 128],
                                         c.identB[:]),
                                     waits=[tns, c.tok_consts,
                                            c.bank_tok.get(bkey)]
                                     if j == 0 else [],
                                     sig="c" if j == jmax - 1 else None)
                        evp = P.e("scalar", lambda e, g4=g4, poff=poff,
                                  jmax=jmax: e.copy(
                                      out=PT[:, g4 * 512:
                                             g4 * 512 + jmax * 128],
                                      in_=c.psumT[:, poff:
                                                  poff + jmax * 128]),
                                  waits=[tt], sig="c")
                        c.bank_tok[bkey] = evp
                        for j in range(jmax):
                            kt = g4 * 4 + j
                            mmo = P.e("tensor", lambda e, kt=kt, j=j,
                                      hp=hp, nkt=nkt: e.matmul(
                                          c.psum[4][0:64, 0:128],
                                          vpb[:, hp * 1024 + kt * 64:
                                              hp * 1024 + kt * 64 + 64],
                                          PT[:, kt * 128:(kt + 1) * 128],
                                          start=(kt == 0),
                                          stop=(kt == nkt - 1)),
                                      waits=[evp, tpl,
                                             c.bank_tok.get("p4")]
                                      if j == 0 else [],
                                      sig="c" if kt == nkt - 1 else None)
                    evo = P.e("vector", lambda e, qt=qt, h=h: e.tensor_copy(
                        out=bass.AP(yfm.tensor,
                                    yfm.offset + (h // 2) * 2048 + qt * 128
                                    + (h % 2) * 64 * A16.n,
                                    [[A16.n, 64], [1, 128]]),
                        in_=c.psum[4][0:64, 0:128]),
                        waits=[mmo], sig="c")
                    c.bank_tok["p4"] = evo
            pair_tok[hp] = mmo
        yfm_ready = c.bank_tok["p4"]
        texp_last = texp
        A16.pop()
        A32.pop()

        A32.push()
        h0t = A32.take(2048)
        x2t = A32.take(2048)
        A16.push()
        wpb = A16.take(8192)
        xhb2 = A16.take(2048)
        h0w_done = [(P.dma_sem(("h0w", i)),
                     P.cnt[P.dma_sem(("h0w", i))], "#") for i in (0, 1)]
        for ocx in range(2):
            tw = P.e("gpsimd", lambda e, ocx=ocx: e.dma_start(
                out=wpb[:, ocx * 4096:(ocx + 1) * 4096],
                in_=bass.AP(ins["L0_ffn_wo"], ocx * 512,
                            [[1024, 128], [128 * 1024, 8], [1, 512]])),
                waits=[wpb_tok[ocx], yfm_ready], sig="dma", dkey=("wp", ocx))
            c.bank_tok["wp%d" % ocx] = tw
        ev_all2 = []
        for it in range(16):
            i = b * 16 + it
            sl = it % 2
            h0i = h0t[:, sl * 1024:(sl + 1) * 1024]
            x2i = x2t[:, sl * 1024:(sl + 1) * 1024]
            xbs = xhb2[:, sl * 1024:(sl + 1) * 1024]
            th0r = P.e("sync", lambda e, i=i, h0i=h0i: e.dma_start(
                out=h0i, in_=h0_d.ap()[i * 128:(i + 1) * 128, :]),
                waits=[prev_h0t[sl]]
                + ((h0w_done + [texp_last]) if it < 2 else []),
                sig="dma", dkey=("h0r", sl))
            evs = []
            for ocx in range(2):
                pb = c.psum[2] if ocx == 0 else c.psum[5]
                bk = "p2" if ocx == 0 else "p5"
                mm = None
                for kd in range(8):
                    mm = P.e("tensor", lambda e, kd=kd, ocx=ocx, it=it,
                             pb=pb: e.matmul(
                                 pb[:, 0:512],
                                 yfm[:, kd * 2048 + it * 128:
                                     kd * 2048 + it * 128 + 128],
                                 wpb[:, ocx * 4096 + kd * 512:
                                     ocx * 4096 + (kd + 1) * 512],
                                 start=(kd == 0), stop=(kd == 7)),
                             waits=[c.bank_tok.get("wp%d" % ocx),
                                    yfm_ready, c.bank_tok.get(bk)]
                             if kd == 0 else [],
                             sig="c" if kd == 7 else None)
                ev = P.e("vector", lambda e, ocx=ocx, pb=pb, h0i=h0i,
                         x2i=x2i: e.tensor_tensor(
                             out=x2i[:, ocx * 512:(ocx + 1) * 512],
                             in0=pb[:, 0:512],
                             in1=h0i[:, ocx * 512:(ocx + 1) * 512],
                             op=ALU.add),
                         waits=[mm, th0r] + (prev_x2use[sl] or []),
                         sig="c")
                c.bank_tok[bk] = ev
                evs.append(ev)
            tx2w = P.e("sync", lambda e, i=i, x2i=x2i: e.dma_start(
                out=x2_d.ap()[i * 128:(i + 1) * 128, :], in_=x2i),
                waits=evs, sig="dma", dkey=("x2w", sl))
            trm2 = ln_tile(P, c, x2i, xbs, evs + [prev_tt2[sl]], slot=sl,
                           rms=True, eps=1e-6)
            prev_x2use[sl] = [trm2, tx2w]
            prev_h0t[sl] = evs[-1]
            for half in range(2):
                poff = 1024 if half == 0 else 0
                bkey = "B" if half == 0 else "A"
                tt = None
                for j in range(4):
                    jj = half * 4 + j
                    tt = P.e("tensor", lambda e, jj=jj, j=j, poff=poff,
                             xbs=xbs: e.transpose(
                                 c.psumT[:, poff + j * 128:
                                         poff + (j + 1) * 128],
                                 xbs[:, jj * 128:(jj + 1) * 128],
                                 c.identB[:]),
                             waits=[trm2, c.tok_consts,
                                    c.bank_tok.get(bkey)]
                             if j == 0 else [],
                             sig="c" if j == 3 else None)
                dsth = bass.AP(h2hat.tensor,
                               h2hat.offset + (half * 4) * 2048 + it * 128,
                               [[A16.n, 128], [2048, 4], [1, 128]])
                ev = P.e("scalar", lambda e, dsth=dsth, poff=poff: e.copy(
                    out=dsth, in_=c.psumT[:, poff:poff + 512]),
                    waits=[tt], sig="c")
                c.bank_tok[bkey] = ev
                ev_all2.append(ev)
                prev_tt2[sl] = tt
        tsc2 = None
        for kd in range(8):
            tsc2 = P.e("vector", lambda e, kd=kd: e.tensor_scalar(
                out=h2hat[:, kd * 2048:(kd + 1) * 2048],
                in0=h2hat[:, kd * 2048:(kd + 1) * 2048],
                scalar1=lnw2f[:, kd:kd + 1], scalar2=None, op0=ALU.mult),
                waits=[ev_all2[-1], tl2] if kd == 0 else [], sig="c")
        gate_toks = []
        gmm = None
        for it in range(16):
            for kd in range(8):
                gmm = P.e("tensor", lambda e, kd=kd, it=it: e.matmul(
                    c.psum[3][:, 0:8],
                    h2hat[:, kd * 2048 + it * 128:
                          kd * 2048 + it * 128 + 128],
                    rwb[:, kd * 8:(kd + 1) * 8],
                    start=(kd == 0), stop=(kd == 7)),
                    waits=[tsc2, trw, c.bank_tok.get("p3")]
                    if kd == 0 else [],
                    sig="c" if kd == 7 else None)
            tl0 = P.e("vector", lambda e: e.tensor_copy(
                out=lgt[:, 0:8], in_=c.psum[3][:, 0:8]),
                waits=[gmm, comb_war], sig="c")
            c.bank_tok["p3"] = tl0
            t1 = P.e("vector", lambda e: e.reduce_max(
                out=lgt[:, 8:9], in_=lgt[:, 0:8], axis=AX),
                waits=[tl0], sig="c")
            t2 = P.e("vector", lambda e: e.tensor_scalar(
                out=lgt[:, 16:24], in0=lgt[:, 0:8], scalar1=lgt[:, 8:9],
                scalar2=None, op0=ALU.is_equal), waits=[t1], sig="c")
            t3 = P.e("vector", lambda e: e.scalar_tensor_tensor(
                out=lgt[:, 24:32], in0=lgt[:, 16:24], scalar=-1e30,
                in1=lgt[:, 0:8], op0=ALU.mult, op1=ALU.add),
                waits=[t2], sig="c")
            t4 = P.e("vector", lambda e: e.reduce_max(
                out=lgt[:, 9:10], in_=lgt[:, 24:32], axis=AX),
                waits=[t3], sig="c")
            t5 = P.e("vector", lambda e: e.tensor_scalar(
                out=lgt[:, 10:11], in0=lgt[:, 8:9], scalar1=lgt[:, 9:10],
                scalar2=None, op0=ALU.subtract), waits=[t4], sig="c")
            t6 = P.e("scalar", lambda e: e.activation(
                out=lgt[:, 11:12], in_=lgt[:, 10:11], func=AF.Sigmoid),
                waits=[t5], sig="c")
            t7 = P.e("vector", lambda e: e.tensor_scalar(
                out=lgt[:, 12:13], in0=lgt[:, 11:12], scalar1=-1.0,
                scalar2=1.0, op0=ALU.mult, op1=ALU.add), waits=[t6],
                sig="c")
            t8 = P.e("vector", lambda e: e.tensor_scalar(
                out=lgt[:, 24:32], in0=lgt[:, 24:32], scalar1=lgt[:, 9:10],
                scalar2=None, op0=ALU.is_equal), waits=[t7], sig="c")
            t9 = P.e("vector", lambda e: e.tensor_scalar(
                out=lgt[:, 16:24], in0=lgt[:, 16:24],
                scalar1=lgt[:, 11:12], scalar2=None, op0=ALU.mult),
                waits=[t8], sig="c")
            t10 = P.e("vector", lambda e: e.tensor_scalar(
                out=lgt[:, 24:32], in0=lgt[:, 24:32],
                scalar1=lgt[:, 12:13], scalar2=None, op0=ALU.mult),
                waits=[t9], sig="c")
            gtok = P.e("vector", lambda e, it=it: e.tensor_tensor(
                out=comb[:, it * 8:(it + 1) * 8], in0=lgt[:, 16:24],
                in1=lgt[:, 24:32], op=ALU.add), waits=[t10], sig="c")
            gate_toks.append(gtok)
        gmm_last = gmm
        A16.pop()
        A32.pop()
        A16.pop()

        A32.push()
        sg32 = A32.take(1024)
        A16.push()
        hfm = A16.take(14336)
        wgb = A16.take(4096)
        wub = A16.take(4096)
        wdb = A16.take(14336)
        wg_tok = [None, None]
        wu_tok = [None, None]
        wd_tok = [None, None]
        sg_war = [None, None]
        FW2 = [gmm_last] + wcast_done
        prev_huse = None
        dcount = 0
        last_ev = None
        for tb in range(2):
            for e_i in range(8):
                for mh in range(2):
                    hev = None
                    mmg = mmu = None
                    for mc in range(7):
                        sbg = mc % 2
                        wgbs = wgb[:, sbg * 2048:(sbg + 1) * 2048]
                        wubs = wub[:, sbg * 2048:(sbg + 1) * 2048]
                        moff = e_i * 1024 * 3584 + mh * 1792 + mc * 256
                        twg = P.e("scalar", lambda e, moff=moff, wgbs=wgbs:
                                  e.dma_start(
                                      out=wgbs,
                                      in_=bass.AP(wg_b, moff,
                                                  [[3584, 128],
                                                   [128 * 3584, 8],
                                                   [1, 256]])),
                                  waits=[wg_tok[sbg]]
                                  + (FW2 if wg_tok[sbg] is None else []),
                                  sig="dma", dkey=("wg", sbg))
                        twu = P.e("sync", lambda e, moff=moff, wubs=wubs:
                                  e.dma_start(
                                      out=wubs,
                                      in_=bass.AP(wu_b, moff,
                                                  [[3584, 128],
                                                   [128 * 3584, 8],
                                                   [1, 256]])),
                                  waits=[wu_tok[sbg]]
                                  + (FW2 if wu_tok[sbg] is None else []),
                                  sig="dma", dkey=("wu", sbg))
                        for mi in range(2):
                            for tcx in range(2):
                                pi = (mi * 2 + tcx) % 2
                                pg = c.psum[2] if pi == 0 else c.psum[3]
                                pu = c.psum[5] if pi == 0 else c.psum[4]
                                bkg = "p2" if pi == 0 else "p3"
                                bku = "p5" if pi == 0 else "p4"
                                for kd in range(8):
                                    mmg = P.e("tensor", lambda e, kd=kd,
                                              mi=mi, tcx=tcx, tb=tb, pg=pg,
                                              wgbs=wgbs: e.matmul(
                                                  pg[:, 0:512],
                                                  wgbs[:, kd * 256
                                                       + mi * 128:
                                                       kd * 256 + mi * 128
                                                       + 128],
                                                  h2hat[:, kd * 2048
                                                        + tb * 1024
                                                        + tcx * 512:
                                                        kd * 2048 + tb * 1024
                                                        + tcx * 512 + 512],
                                                  start=(kd == 0),
                                                  stop=(kd == 7)),
                                              waits=[twg, tsc2,
                                                     c.bank_tok.get(bkg)]
                                              if kd == 0 else [],
                                              sig="c" if kd == 7 else None)
                                for kd in range(8):
                                    mmu = P.e("tensor", lambda e, kd=kd,
                                              mi=mi, tcx=tcx, tb=tb, pu=pu,
                                              wubs=wubs: e.matmul(
                                                  pu[:, 0:512],
                                                  wubs[:, kd * 256
                                                       + mi * 128:
                                                       kd * 256 + mi * 128
                                                       + 128],
                                                  h2hat[:, kd * 2048
                                                        + tb * 1024
                                                        + tcx * 512:
                                                        kd * 2048 + tb * 1024
                                                        + tcx * 512 + 512],
                                                  start=(kd == 0),
                                                  stop=(kd == 7)),
                                              waits=[twu, tsc2,
                                                     c.bank_tok.get(bku)]
                                              if kd == 0 else [],
                                              sig="c" if kd == 7 else None)
                                evsg = P.e("scalar", lambda e, pg=pg, pi=pi:
                                           e.activation(
                                               out=sg32[:, pi * 512:
                                                        (pi + 1) * 512],
                                               in_=pg[:, 0:512],
                                               func=AF.Silu),
                                           waits=[mmg, sg_war[pi]],
                                           sig="c")
                                c.bank_tok[bkg] = evsg
                                kfl = mc * 2 + mi
                                evm = P.e("vector", lambda e, kfl=kfl,
                                          tcx=tcx, pu=pu, pi=pi:
                                          e.tensor_tensor(
                                              out=hfm[:, kfl * 1024
                                                      + tcx * 512:
                                                      kfl * 1024 + tcx * 512
                                                      + 512],
                                              in0=sg32[:, pi * 512:
                                                       (pi + 1) * 512],
                                              in1=pu[:, 0:512],
                                              op=ALU.mult),
                                          waits=[evsg, mmu, prev_huse],
                                          sig="c")
                                c.bank_tok[bku] = evm
                                sg_war[pi] = evm
                                hev = evm
                        wg_tok[sbg] = mmg
                        wu_tok[sbg] = mmu
                    h_ready = hev
                    h_last_mm = None
                    for ocx in range(2):
                        sbd = dcount % 2
                        dcount += 1
                        wdbs = wdb[:, sbd * 7168:(sbd + 1) * 7168]
                        twd = P.e("sync", lambda e, e_i=e_i, mh=mh, ocx=ocx,
                                  wdbs=wdbs: e.dma_start(
                                      out=wdbs,
                                      in_=bass.AP(wd_b,
                                                  e_i * 3584 * 1024
                                                  + mh * 1792 * 1024
                                                  + ocx * 512,
                                                  [[1024, 128],
                                                   [128 * 1024, 14],
                                                   [1, 512]])),
                                  waits=[wd_tok[sbd]]
                                  + (FW2 if wd_tok[sbd] is None else []),
                                  sig="dma", dkey=("wd", sbd))
                        for st in range(8):
                            pb = c.psum[2] if st % 2 == 0 else c.psum[5]
                            bk = "p2" if st % 2 == 0 else "p5"
                            mm = None
                            for kf in range(14):
                                mm = P.e("tensor", lambda e, kf=kf, st=st,
                                         pb=pb, wdbs=wdbs: e.matmul(
                                             pb[:, 0:512],
                                             hfm[:, kf * 1024 + st * 128:
                                                 kf * 1024 + st * 128
                                                 + 128],
                                             wdbs[:, kf * 512:
                                                  (kf + 1) * 512],
                                             start=(kf == 0),
                                             stop=(kf == 13)),
                                         waits=[twd, h_ready,
                                                c.bank_tok.get(bk)]
                                         if kf == 0 else [],
                                         sig="c" if kf == 13 else None)
                            slot = (st % 2) * 2 + ocx
                            ev = P.e("vector", lambda e, st=st, tb=tb,
                                     e_i=e_i, pb=pb, slot=slot:
                                     e.tensor_scalar(
                                         out=stg32(c)[:, slot * 512:
                                                      slot * 512 + 512],
                                         in0=pb[:, 0:512],
                                         scalar1=comb[:, (tb * 8 + st) * 8
                                                      + e_i:
                                                      (tb * 8 + st) * 8
                                                      + e_i + 1],
                                         scalar2=None, op0=ALU.mult),
                                     waits=[mm, gate_toks[tb * 8 + st],
                                            acc_tok.get((st % 2, ocx))],
                                     sig="c")
                            c.bank_tok[bk] = ev
                            last_ev = ev
                            dst = acc1_d.ap()[b * T + tb * 1024 + st * 128:
                                              b * T + tb * 1024
                                              + (st + 1) * 128,
                                              ocx * 512:(ocx + 1) * 512]
                            if e_i == 0 and mh == 0:
                                adm = P.e("gpsimd", lambda e, dst=dst,
                                          slot=slot: e.dma_start(
                                              out=dst,
                                              in_=stg32(c)[:, slot * 512:
                                                           slot * 512
                                                           + 512]),
                                          waits=[ev], sig="dma",
                                          dkey=("acc", st % 2, ocx))
                            else:
                                adm = P.e("gpsimd", lambda e, dst=dst,
                                          slot=slot: e.dma_start(
                                              out=dst,
                                              in_=stg32(c)[:, slot * 512:
                                                           slot * 512
                                                           + 512],
                                              accum_op=ALU.add),
                                          waits=[ev], sig="dma",
                                          dkey=("acc", st % 2, ocx))
                            acc_tok[(st % 2, ocx)] = adm
                            h_last_mm = mm
                        wd_tok[sbd] = h_last_mm
                    prev_huse = h_last_mm
        comb_war = last_ev
        A16.pop()
        A32.pop()

        A32.push()
        ya2 = A32.take(2048)
        x2r = A32.take(2048)
        xinF = A32.take(2048)
        x2w_done = [(P.dma_sem(("x2w", i)),
                     P.cnt[P.dma_sem(("x2w", i))], "#") for i in (0, 1)]
        for it in range(16):
            i = b * 16 + it
            sl = it % 2
            ya = ya2[:, sl * 1024:(sl + 1) * 1024]
            yb = x2r[:, sl * 1024:(sl + 1) * 1024]
            xi = xinF[:, sl * 1024:(sl + 1) * 1024]
            tl_ = P.e("sync", lambda e, i=i, ya=ya: e.dma_start(
                out=ya, in_=acc1_d.ap()[i * 128:(i + 1) * 128, :]),
                waits=[prevF[sl], acc_tok.get((it % 2, 0)),
                       acc_tok.get((it % 2, 1))],
                sig="dma", dkey=("accr", sl))
            tx2r = P.e("scalar", lambda e, i=i, yb=yb: e.dma_start(
                out=yb, in_=x2_d.ap()[i * 128:(i + 1) * 128, :]),
                waits=[prevF2[sl]] + (x2w_done if it < 2 else []),
                sig="dma", dkey=("x2r", sl))
            txi = P.e("sync", lambda e, i=i, xi=xi: e.dma_start(
                out=xi, in_=x_src.ap()[i * 128:(i + 1) * 128, :]),
                waits=[prevF3[sl]], sig="dma", dkey=("xinF", sl))
            ta = P.e("vector", lambda e, ya=ya, yb=yb: e.tensor_tensor(
                out=ya, in0=ya, in1=yb, op=ALU.add),
                waits=[tl_, tx2r], sig="c")
            t3 = ln_tile(P, c, ya, yb, [ta, prevF2[sl]], slot=sl)
            t4 = P.e("vector", lambda e, yb=yb, xi=xi: e.tensor_tensor(
                out=yb, in0=yb, in1=xi, op=ALU.add),
                waits=[t3, txi], sig="c")
            stt = P.e("sync", lambda e, i=i, yb=yb: e.dma_start(
                out=x_dst.ap()[i * 128:(i + 1) * 128, :], in_=yb),
                waits=[t4], sig="dma", dkey=("xout", sl))
            prevF[sl] = t3
            prevF2[sl] = stt
            prevF3[sl] = t4
        A32.pop()
    A32.pop()
    A16.pop()
    P.barrier()
    c.arena_war = None



INPUT_SPECS = {
    "L0_attn_x": (16, 2048, 1024), "L0_attn_Wqkv": (1024, 3072),
    "L0_attn_bqkv": (3072,), "L0_attn_Wproj": (1024, 1024),
    "L0_attn_bproj": (1024,), "L0_attn_w1": (1, 32), "L0_attn_b1": (32,),
    "L0_attn_w2": (32, 16), "L0_attn_b2": (16,), "L0_attn_c_param": (),
    "L0_attn_L_multiplier": (), "L0_ffn_ln1_w": (1024,),
    "L0_ffn_ln2_w": (1024,), "L0_ffn_wq": (1024, 1024),
    "L0_ffn_wk": (1024, 512), "L0_ffn_wv": (1024, 512),
    "L0_ffn_wo": (1024, 1024), "L0_ffn_router_w": (1024, 8),
    "L0_ffn_w_gate": (8, 1024, 3584), "L0_ffn_w_up": (8, 1024, 3584),
    "L0_ffn_w_down": (8, 3584, 1024), "L1_attn_qkv_w": (3072, 1024),
    "L1_attn_proj_w": (1024, 1024), "L1_attn_alpha_log": (16,),
    "L1_attn_beta_log": (16,), "L1_ffn_weight": (1024, 1024),
    "L1_ffn_alpha": (1,), "L1_ffn_bias": (1024,), "L2_attn_Wqkv": (3072, 1024),
    "L2_attn_bqkv": (3072,), "L2_attn_Wr": (257, 64), "L2_attn_br": (257,),
    "L2_ffn_weight": (1024, 1024), "L2_ffn_alpha": (1,), "L2_ffn_bias": (1024,),
    "L3_attn_conv_weight": (1024, 1, 4), "L3_attn_conv_bias": (1024,),
    "L3_attn_Wr_w": (1024, 1024), "L3_attn_Wr_b": (1024,),
    "L3_attn_Wi_w": (1024, 1024), "L3_attn_Wi_b": (1024,),
    "L3_attn_log_a": (1024,), "L3_attn_out_w": (1024, 1024),
    "L3_attn_norm_w": (1024,), "L3_ffn_gate_w": (8, 1024),
    "L3_ffn_w1": (8, 4096, 1024), "L3_ffn_w2": (8, 1024, 4096),
}

HB_INPUTS = {
    0: [k for k in INPUT_SPECS if k.startswith("L0_attn") and k != "L0_attn_x"],
    1: [k for k in INPUT_SPECS if k.startswith("L0_ffn")],
    2: [k for k in INPUT_SPECS if k.startswith("L1_attn")],
    3: [k for k in INPUT_SPECS if k.startswith("L1_ffn")],
    4: [k for k in INPUT_SPECS if k.startswith("L2_attn")],
    5: [k for k in INPUT_SPECS if k.startswith("L2_ffn")],
    6: [k for k in INPUT_SPECS if k.startswith("L3_attn")],
    7: [k for k in INPUT_SPECS if k.startswith("L3_ffn")],
}

HB_EMIT = {}


def emit_hb(P, c, b, x_src, x_dst):
    if b == 3:
        emit_ternary_ffn(P, c, "L1_ffn_", x_src, x_dst)
    elif b == 5:
        emit_ternary_ffn(P, c, "L2_ffn_", x_src, x_dst)
    elif b == 6:
        emit_l3_attn(P, c, x_src, x_dst)
    elif b == 2:
        emit_l1_attn(P, c, x_src, x_dst)
    elif b == 7:
        emit_l3_ffn(P, c, x_src, x_dst)
    elif b == 4:
        emit_l2_attn(P, c, x_src, x_dst)
    elif b == 0:
        emit_l0_attn(P, c, x_src, x_dst)
    elif b == 1:
        emit_l0_ffn(P, c, x_src, x_dst)
    elif b in HB_EMIT:
        HB_EMIT[b](P, c, x_src, x_dst)
    else:
        raise NotImplementedError(f"hb{b}")


def build_program(blocks):
    nc = bass.Bass("TRN2", target_bir_lowering=False, debug=False)
    P = Prog(nc)
    c = Ctx()
    c.ln_war = {}
    c.bank_tok = {}
    c.const_cache = {}
    c.arena_war = None
    c.ins = {}
    for b in blocks:
        for k in HB_INPUTS[b]:
            shp = list(INPUT_SPECS[k]) or [1]
            c.ins[k] = _dram(nc, k, shp, F32, kind="ExternalInput")
    x_in = _dram(nc, "x", (TOK, D), F32, kind="ExternalInput")
    y_out = _dram(nc, "y", (TOK, D), F32, kind="ExternalOutput")
    xA = _dram(nc, "xA", (TOK, D), F32, kind="Internal")
    xB = _dram(nc, "xB", (TOK, D), F32, kind="Internal")
    c.scr = {}

    c.psum = [nc.alloc_psum_tensor(f"pb{i}", [128, 512], F32) for i in range(6)]
    c.psumT = nc.alloc_psum_tensor("pT", [128, 2048], BF16)
    c.cpool = nc.alloc_sbuf_tensor("cpool", [128, 32], F32)
    c.ln_st = [nc.alloc_sbuf_tensor(f"ln_st{i}", [128, 8], F32)
               for i in range(2)]
    c.ln_sq = [nc.alloc_sbuf_tensor(f"ln_sq{i}", [128, 1024], F32)
               for i in range(2)]
    c.tmpA = nc.alloc_sbuf_tensor("tmpA", [128, 1024], F32)
    c.tmpB = nc.alloc_sbuf_tensor("tmpB", [128, 1024], F32)
    c.tmp2k = nc.alloc_sbuf_tensor("tmp2k", [128, 2048], F32)
    c.A32 = Arena(nc, "A32", 44 * 1024, F32)
    c.A16 = Arena(nc, "A16", 124 * 1024, BF16)

    with contextlib.ExitStack() as stack:
        emit_consts(P, c)

        cur = x_in
        pool = [xA, xB]
        for idx, b in enumerate(blocks):
            dst = y_out if idx == len(blocks) - 1 else pool[idx % 2]
            emit_hb(P, c, b, cur, dst)
            cur = dst

        P.barrier()

        for sname in P.cnt:
            P.sems[sname] = stack.enter_context(nc.semaphore(sname))
        with nc.Block() as block:
            P.build_into(block)
        with nc.Block() as block2:
            def _clear(e):
                for sname in P.cnt:
                    e.sem_clear(P.sems[sname])
            block2.gpsimd(_clear)
    return nc


def run_blocks(blocks, x_full, inputs, trace=False):
    nc = build_program(blocks)
    in_maps = []
    for core in range(NCORES):
        m = {"x": np.ascontiguousarray(
            x_full[2 * core:2 * core + 2].reshape(TOK, D).astype(np.float32))}
        for b in blocks:
            for k in HB_INPUTS[b]:
                arr = np.asarray(inputs[k], dtype=np.float32)
                if arr.ndim == 0:
                    arr = arr.reshape(1)
                m[k] = np.ascontiguousarray(arr)
        in_maps.append(m)
    res = run_bass_kernel_spmd(nc, in_maps, core_ids=list(range(NCORES)),
                               trace=trace)
    out = np.stack([r["y"].reshape(2, T, D) for r in res.results], axis=0)
    return out.reshape(16, T, D), res


def kernel(**inputs):
    x = np.asarray(inputs["L0_attn_x"], dtype=np.float32)
    out, _ = run_blocks(list(range(8)), x, inputs)
    return out.astype(np.float32)
